# version 16
# baseline (speedup 1.0000x reference)
import math
from contextlib import ExitStack

import numpy as np

import concourse.bass as bass
import concourse.mybir as mybir
from concourse.bass_utils import run_bass_kernel_spmd

F32 = mybir.dt.float32
BF16 = mybir.dt.bfloat16
AF = mybir.ActivationFunctionType
ALU = mybir.AluOpType
AX = mybir.AxisListType

S, D, H, QL, KVL, DN, DR, DV, NE, NR = 2048, 4096, 32, 1536, 512, 128, 64, 128, 16, 64
SPAN, NSPAN, CH = 512, 4, 128
OSP, NOSP = 256, 16
SCALE = 1.0 / math.sqrt(192.0)
EPS = 1e-5
NCORES = 8
ENGS = ("pe", "act", "dve", "po", "sp")


def _round32(x):
    return (x + 31) & ~31


class _Sem:
    def __init__(self, es, nc, name, is_queue=False):
        self.h = es.enter_context(nc.semaphore(name))
        self.name = name
        self.n = 0
        self.is_queue = is_queue

    def bump(self, k):
        self.n += k
        return self.n


class _Ovl:

    def __init__(self, nc, total):
        self.nc = nc
        self.total = total
        self.off = 0
        self.stack = []
        self.tensors = []

    def alloc(self, name, shape, dtype):
        per = int(np.prod(shape[1:])) * mybir.dt.size(dtype)
        off = _round32(self.off)
        self.off = off + per
        assert self.off <= self.total, (name, self.off, self.total)
        h = self.nc.virtual_sbuf_tensor(name, list(shape), dtype)
        self.tensors.append((h, off))
        return h

    def push(self):
        self.stack.append(self.off)

    def pop(self):
        self.off = self.stack.pop()

    def finalize(self, es):
        backing = es.enter_context(
            self.nc.sbuf_tensor("ovl_backing", [128, self.total], mybir.dt.uint8))
        base = self.nc.lookup_mloc(backing).addr
        for h, off in self.tensors:
            self.nc._tensor(h.name, h.shape, h.dtype, type="SB")
            mloc = self.nc.lookup_mloc(h)
            mloc.allocated = not self.nc.target_bir_lowering
            mloc.addr = base + off


class _Prog:

    def __init__(self):
        self.q = {e: [] for e in ENGS}
        self.wm = {}

    def wait(self, eng, tok):
        if tok is None:
            return
        sem, v = tok
        if getattr(sem, "is_queue", False):
            v = sem.n
        if self.wm.get((eng, sem.name), 0) >= v:
            return
        self.wm[(eng, sem.name)] = v
        self.q[eng].append(lambda E, h=sem.h, vv=v: E.wait_ge(h, vv))

    def op(self, eng, build, sem=None, k=1, waits=()):
        for t in waits:
            self.wait(eng, t)
        if sem is None:
            self.q[eng].append(build)
            return None
        v = sem.bump(k)
        self.q[eng].append(lambda E, b=build, h=sem.h, kk=k: b(E).then_inc(h, kk))
        return (sem, v)

    def barrier(self, sems):
        toks = [(s, s.n) for s in sems]
        for e in ENGS:
            for t in toks:
                self.wait(e, t)


def build_nc(tap=None):
    nc = bass.Bass("TRN2", target_bir_lowering=False)

    x = nc.dram_tensor("x", [S, D], F32, kind="ExternalInput")
    wqd = nc.dram_tensor("wqd", [D, QL], F32, kind="ExternalInput")
    wqun = nc.dram_tensor("wqun", [H, 128, QL], F32, kind="ExternalInput")
    wqur = nc.dram_tensor("wqur", [H, 128, 12 * DR], F32, kind="ExternalInput")
    wkv = nc.dram_tensor("wkv", [D, KVL], F32, kind="ExternalInput")
    wkr = nc.dram_tensor("wkr", [D, DR], F32, kind="ExternalInput")
    wuk = nc.dram_tensor("wuk", [H * DN, KVL], F32, kind="ExternalInput")
    wuvT = nc.dram_tensor("wuvT", [H, 128, 4 * DV], F32, kind="ExternalInput")
    wout = nc.dram_tensor("wout", [NOSP, 128, 32 * OSP], F32, kind="ExternalInput")
    bwT = nc.dram_tensor("bwT", [NOSP, 128, 32 * OSP], F32, kind="ExternalInput")
    bst = nc.dram_tensor("bst", [NOSP, 128, 8 * OSP], F32, kind="ExternalInput")
    bb = nc.dram_tensor("bb", [1, D], F32, kind="ExternalInput")
    at_ = nc.dram_tensor("at_", [D, NR], F32, kind="ExternalInput")
    rt = nc.dram_tensor("rt", [D, NE], F32, kind="ExternalInput")
    ident = nc.dram_tensor("ident", [128, 128], F32, kind="ExternalInput")
    csqh = nc.dram_tensor("csqh", [DR, S], F32, kind="ExternalInput")
    csql = nc.dram_tensor("csql", [DR, S], F32, kind="ExternalInput")
    snqh = nc.dram_tensor("snqh", [DR, S], F32, kind="ExternalInput")
    snql = nc.dram_tensor("snql", [DR, S], F32, kind="ExternalInput")
    cskh = nc.dram_tensor("cskh", [DR, S], F32, kind="ExternalInput")
    cskl = nc.dram_tensor("cskl", [DR, S], F32, kind="ExternalInput")
    snkh = nc.dram_tensor("snkh", [DR, S], F32, kind="ExternalInput")
    snkl = nc.dram_tensor("snkl", [DR, S], F32, kind="ExternalInput")
    maskd = nc.dram_tensor("maskd", [4, 128, SPAN], F32, kind="ExternalInput")
    indd = nc.dram_tensor("indd", [NE, NE * NR], F32, kind="ExternalInput")

    scr_kind = "ExternalOutput" if tap else "Internal"
    wqd_bf = nc.dram_tensor("wqd_bf", [128, 32, QL], BF16, kind="Internal")
    wkv_bf = nc.dram_tensor("wkv_bf", [128, 32, KVL], BF16, kind="Internal")
    bwT_bf = nc.dram_tensor("bwT_bf", [NOSP, 128, 32 * OSP], BF16, kind="Internal")
    wout_bf = nc.dram_tensor("wout_bf", [NOSP, 128, 32 * OSP], BF16, kind="Internal")
    bst_bf = nc.dram_tensor("bst_bf", [NOSP, 128, 8 * OSP], BF16, kind="Internal")
    xn_scr = nc.dram_tensor("xn_scr", [NSPAN, 128, 32 * SPAN], BF16, kind=scr_kind)
    am_scr = nc.dram_tensor("am_scr", [NSPAN, 128, 32 * SPAN], BF16, kind=scr_kind)
    g_scr = nc.dram_tensor("g_scr", [NSPAN, 128, 8 * SPAN], BF16, kind=scr_kind)

    out = nc.dram_tensor("out", [S, D], F32, kind="ExternalOutput")
    if tap:
        tap_cq = nc.dram_tensor("tap_cq", [128, 12, S], BF16, kind="ExternalOutput")
        tap_ckv = nc.dram_tensor("tap_ckv", [128, 4, S], BF16, kind="ExternalOutput")
        tap_ckvt = nc.dram_tensor("tap_ckvt", [128, 16, KVL], BF16, kind="ExternalOutput")
        tap_kro = nc.dram_tensor("tap_kro", [DR, S], BF16, kind="ExternalOutput")
        tap_q = nc.dram_tensor("tap_q", [128, 16], F32, kind="ExternalOutput")
        tap_sm = nc.dram_tensor("tap_sm", [128, 24], F32, kind="ExternalOutput")
        tap_ge1 = nc.dram_tensor("tap_ge1", [128, 16], F32, kind="ExternalOutput")
        tap_ge2 = nc.dram_tensor("tap_ge2", [128, 16], F32, kind="ExternalOutput")
        tap_cmb = nc.dram_tensor("tap_cmb", [128, 16], F32, kind="ExternalOutput")
        tap_lg = nc.dram_tensor("tap_lg", [128, 64], F32, kind="ExternalOutput")
        tap_qn = nc.dram_tensor("tap_qn", [128, S], BF16, kind="ExternalOutput")
        tap_qro = nc.dram_tensor("tap_qro", [DR, S], BF16, kind="ExternalOutput")
        tap_qlat = nc.dram_tensor("tap_qlat", [128, 4, S], BF16, kind="ExternalOutput")
        tap_expT = nc.dram_tensor("tap_expT", [128, 16, SPAN], BF16, kind="ExternalOutput")
        tap_rs = nc.dram_tensor("tap_rs", [1, SPAN], F32, kind="ExternalOutput")
        tap_ol = nc.dram_tensor("tap_ol", [128, 4, SPAN], BF16, kind="ExternalOutput")

    es = ExitStack()
    with es:
        pb = [es.enter_context(nc.psum_tensor(f"pb{i}", [128, 512], F32))
              for i in range(7)]
        pbT = es.enter_context(nc.psum_tensor("pbT", [128, 1024], BF16))

        ovl = _Ovl(nc, 212480)
        ident_f = ovl.alloc("ident_f", [128, 128], F32)
        ident_b = ovl.alloc("ident_b", [128, 128], BF16)
        ones_col = ovl.alloc("ones_col", [128, 1], BF16)
        ones_l1f = ovl.alloc("ones_l1f", [1, 128], F32)
        ones_l1b = ovl.alloc("ones_l1b", [1, 128], BF16)
        eps_sb = ovl.alloc("eps_sb", [128, 1], F32)
        ovl.push()
        c_q = ovl.alloc("c_q", [128, 12, S], BF16)
        ckv_fm = ovl.alloc("ckv_fm", [128, 4, S], BF16)
        krope = ovl.alloc("krope", [DR, S], BF16)
        ovl.push()
        xn_span = ovl.alloc("xn_span", [128, 32, SPAN], BF16)
        xch = [ovl.alloc(f"xch{i}", [128, D], F32) for i in range(2)]
        sqxn2 = [ovl.alloc(f"sqxn{i}", [128, D], BF16) for i in range(2)]
        wqd_st = [ovl.alloc(f"wqd_st{i}", [128, 2, 768], BF16) for i in range(2)]
        wkv_st = [ovl.alloc(f"wkv_st{i}", [128, 2, KVL], BF16) for i in range(2)]
        wkr_sb = ovl.alloc("wkr_sb", [128, 32, DR], BF16)
        wkr_rot = ovl.alloc("wkr_rot", [128, 32, DR], BF16)
        at_sb = ovl.alloc("at_sb", [128, 32, NR], BF16)
        rt_sb = ovl.alloc("rt_sb", [128, 32, NE], BF16)
        csk = ovl.alloc("csk", [DR, S], F32)
        snk = ovl.alloc("snk", [DR, S], F32)
        ind_sb = ovl.alloc("ind_sb", [NE, NE * NR], F32)
        res2 = ovl.alloc("res2", [128, SPAN], F32)
        g_span = ovl.alloc("g_span", [128, 8, SPAN], BF16)
        lg_sb = ovl.alloc("lg_sb", [128, 64], F32)
        lg_fm = ovl.alloc("lg_fm", [16, SPAN], F32)
        e_sb = ovl.alloc("e_sb", [128, 16], F32)
        q_sb = ovl.alloc("q_sb", [128, 16], F32)
        ge1_sb = ovl.alloc("ge1_sb", [128, 16], F32)
        qm_sb = e_sb
        ge2_sb = ovl.alloc("ge2_sb", [128, 16], F32)
        cmb_sb = ovl.alloc("cmb_sb", [128, 16], F32)
        cmb2_sb = ovl.alloc("cmb2_sb", [128, 64], F32)
        combT = ovl.alloc("combT", [NE, SPAN], F32)
        small = ovl.alloc("small", [128, 24], F32)
        p1_high = ovl.off
        ovl.pop()
        ovl.push()
        ckv_tok = ovl.alloc("ckv_tok", [128, 16, KVL], BF16)
        tlo2 = ovl.alloc("tlo2", [DR, S], F32)
        csq = ovl.alloc("csq", [DR, S], F32)
        snq = ovl.alloc("snq", [DR, S], F32)
        mask_sb = ovl.alloc("mask_sb", [128, 4, SPAN], BF16)
        wqun_st = [ovl.alloc(f"wqun_st{i}", [128, QL], BF16) for i in range(2)]
        wqq_st = [ovl.alloc(f"wqq_st{i}", [128, 12 * 128], BF16) for i in range(2)]
        wuk_st = [ovl.alloc(f"wuk_st{i}", [128, KVL], BF16) for i in range(2)]
        wuv_st = [ovl.alloc(f"wuv_st{i}", [128, 4 * DV], BF16) for i in range(2)]
        qn_sb = ovl.alloc("qn_sb", [128, S], BF16)
        qro_sb = ovl.alloc("qro_sb", [DR, S], BF16)
        qlat = ovl.alloc("qlat", [128, 4, S], BF16)
        expT = ovl.alloc("expT", [128, 16, SPAN], BF16)
        ol_sb = ovl.alloc("ol_sb", [128, 4, SPAN], BF16)
        rs_tok = ovl.alloc("rs_tok", [128, 4], F32)
        rs_row = ovl.alloc("rs_row", [1, SPAN], F32)
        rsbc = ovl.alloc("rsbc", [128, SPAN], F32)
        am_tile = [ovl.alloc(f"am_tile{i}", [128, SPAN], BF16) for i in range(2)]
        qt1 = ovl.alloc("qt1", [DR, SPAN], F32)
        qt2 = ovl.alloc("qt2", [DR, SPAN], F32)
        p2_high = ovl.off
        ovl.pop()
        ovl.pop()
        xnq_sb = ovl.alloc("xnq_sb", [128, 32, SPAN], BF16)
        amq_sb = ovl.alloc("amq_sb", [128, 32, SPAN], BF16)
        gq_sb = ovl.alloc("gq_sb", [128, 8, SPAN], BF16)
        bwT_st = [ovl.alloc(f"bwT_st{i}", [128, 32, OSP], BF16) for i in range(2)]
        wout_st = [ovl.alloc(f"wout_st{i}", [128, 32, OSP], BF16) for i in range(2)]
        bst_st = [ovl.alloc(f"bst_st{i}", [128, 8, OSP], BF16) for i in range(2)]
        bb_sb = ovl.alloc("bb_sb", [1, D], BF16)
        xres = [ovl.alloc(f"xres{i}", [128, OSP], F32) for i in range(4)]
        out_sb = [ovl.alloc(f"out_sb{i}", [128, OSP], F32) for i in range(4)]
        p3_high = ovl.off
        ovl.finalize(es)

        PE = _Sem(es, nc, "s_pe")
        ACTS = _Sem(es, nc, "s_act")
        DVE = _Sem(es, nc, "s_dve")
        QSP = _Sem(es, nc, "q_sp", is_queue=True)
        QC = _Sem(es, nc, "q_c", is_queue=True)
        QCP = _Sem(es, nc, "q_cp", is_queue=True)
        QT = [_Sem(es, nc, f"q_t{i}", is_queue=True) for i in range(2)]
        QW = [_Sem(es, nc, f"q_w{i}", is_queue=True) for i in range(2)]
        QX = [_Sem(es, nc, f"q_x{i}", is_queue=True) for i in range(2)]
        ALLSEMS = [PE, ACTS, DVE, QSP, QC, QCP] + QT + QW + QX

        P = _Prog()

        def MM(outp, lhsT, rhs, start, stop):
            return lambda E: E.matmul(outp, lhsT, rhs, start=start, stop=stop)

        def DMA(dst, src):
            return lambda E: E.dma_start(out=dst, in_=src)

        t_id = P.op("sp", DMA(ident_f[:, :], ident[:, :]), QC, 16)
        t_idb = P.op("po", DMA(ident_b[:, :], ident[:, :]), QCP, 16)
        P.op("dve", lambda E: E.memset(ones_col[:, :], 1.0), DVE)
        P.op("dve", lambda E: E.memset(ones_l1f[0:1, :], 1.0), DVE)
        P.op("dve", lambda E: E.memset(ones_l1b[0:1, :], 1.0), DVE)
        t_ones = P.op("dve", lambda E: E.memset(eps_sb[:, :], EPS), DVE)
        tlo = xch[0]
        rop1 = xch[0][0:64, 1024:1536]
        rop2 = xch[0][0:64, 1536:2048]
        tb_toks = []
        for qi, (hi, lo, dstt) in enumerate(((cskh, cskl, csk), (snkh, snkl, snk))):
            t1 = P.op("sp", DMA(dstt[:, :], hi[:, :]), QT[qi], 16)
            t2 = P.op("sp", DMA(tlo[0:DR, 0:S], lo[:, :]), QT[qi], 16,
                      waits=tb_toks[-1:])
            tb_toks.append(P.op("dve", lambda E, d=dstt: E.tensor_add(d[:, :], d[:, :], tlo[0:DR, 0:S]),
                                DVE, waits=[t1, t2] + tb_toks[-1:]))
        t_ktab = tb_toks[-1]
        wkr_r = wkr[:, :].rearrange("(g p) n -> p g n", p=128)
        at_r = at_[:, :].rearrange("(g p) n -> p g n", p=128)
        rt_r = rt[:, :].rearrange("(g p) n -> p g n", p=128)
        t_wkr = P.op("po", DMA(wkr_sb[:, :, :], wkr_r), QCP, 16)
        t_at = P.op("po", DMA(at_sb[:, :, :], at_r), QCP, 16)
        t_rt = P.op("po", DMA(rt_sb[:, :, :], rt_r), QCP, 16)
        t_ind = P.op("sp", DMA(ind_sb[:, :], indd[:, :]), QC, 16)
        wqd_r = wqd[:, :].rearrange("(g p) n -> p g n", p=128)
        wkv_r = wkv[:, :].rearrange("(g p) n -> p g n", p=128)
        t_wqdc = P.op("po", DMA(wqd_bf[:, :, :], wqd_r), QCP, 16)
        t_wkvc = P.op("po", DMA(wkv_bf[:, :, :], wkv_r), QCP, 16)
        t_rot1 = P.op("dve", lambda E: E.tensor_scalar_mul(
            wkr_rot[:, :, 0:32], wkr_sb[:, :, 32:64], -1.0), DVE, waits=[t_wkvc])
        t_rot2 = P.op("dve", lambda E: E.tensor_copy(
            wkr_rot[:, :, 32:64], wkr_sb[:, :, 0:32]), DVE, waits=[t_wkvc])

        tok_xn = [None] * NSPAN
        tok_cq_done = None
        ln_tp_tok = [None, None]
        ckv_evac = {}
        wkv_rd = [None, None]
        wqd_rd = [None, None]
        t_logit_ev = None
        t_res_ev = None
        g_spill = [None] * NSPAN
        xn_spill = [None] * NSPAN
        t_krope_dve = None
        tp_comb_prev = None
        g_dve_last = None
        tok_cq_done = None
        cq_ev = None
        passA_last = None
        prev_tp = [None, None]
        prev_t_xn = [None, None]
        prev_t_sq = [None, None]

        for sp in range(NSPAN):
            for cs in range(4):
                tc16 = sp * 4 + cs
                slot = tc16 % 2
                c0 = slot * 6
                r0 = tc16 * CH
                sq_b = sqxn2[slot]
                t_x = P.op("sp", DMA(xch[slot][:, :], x[r0:r0 + CH, :]), QX[slot], 16,
                           waits=[ln_tp_tok[0], ln_tp_tok[1], t_ktab, t_krope_dve])
                t_nm = P.op("dve", lambda E, sl=slot, c0=c0: E.reduce_sum(
                    small[:, c0:c0 + 1], xch[sl][:, :], axis=AX.X, negate=True), DVE,
                    waits=[t_x, prev_t_xn[slot], prev_t_sq[slot]])
                t_sq = P.op("act", lambda E, sl=slot, c0=c0, sq_b=sq_b: E.activation(
                    sq_b[:, :], xch[sl][:, :], AF.Square,
                    accum_out=small[:, c0 + 2:c0 + 3]),
                    ACTS, waits=[t_x, prev_tp[slot], prev_t_xn[slot], t_krope_dve])
                prev_t_sq[slot] = t_sq
                t_nm = P.op("dve", lambda E, c0=c0: E.tensor_scalar_mul(
                    small[:, c0 + 1:c0 + 2], small[:, c0:c0 + 1], 1.0 / D), DVE, waits=[t_nm])
                t_v = P.op("dve", lambda E, c0=c0: E.tensor_scalar_mul(
                    small[:, c0 + 3:c0 + 4], small[:, c0 + 2:c0 + 3], 1.0 / D),
                    DVE, waits=[t_sq])
                t_m2 = P.op("dve", lambda E, c0=c0: E.tensor_scalar(
                    small[:, c0:c0 + 1], small[:, c0 + 1:c0 + 2],
                    small[:, c0 + 1:c0 + 2], None, ALU.mult), DVE, waits=[t_nm])
                t_v = P.op("dve", lambda E, c0=c0: E.tensor_sub(
                    small[:, c0 + 2:c0 + 3], small[:, c0 + 3:c0 + 4],
                    small[:, c0:c0 + 1]), DVE, waits=[t_v, t_m2])
                t_sd = P.op("act", lambda E, c0=c0: E.activation(
                    small[:, c0 + 4:c0 + 5], small[:, c0 + 2:c0 + 3], AF.Sqrt,
                    bias=eps_sb[:, 0:1]), ACTS, waits=[t_v, t_ones])
                t_rs = P.op("dve", lambda E, c0=c0: E.reciprocal(
                    small[:, c0 + 5:c0 + 6], small[:, c0 + 4:c0 + 5]), DVE, waits=[t_sd])
                t_xnh = [None, None]
                for hfa in range(2):
                    t_xnh[hfa] = P.op("dve", lambda E, sl=slot, c0=c0, sq_b=sq_b, hf=hfa:
                        E.tensor_scalar(
                            sq_b[:, hf * 2048:(hf + 1) * 2048],
                            xch[sl][:, hf * 2048:(hf + 1) * 2048],
                            small[:, c0 + 1:c0 + 2], small[:, c0 + 5:c0 + 6],
                            ALU.add, ALU.mult), DVE, waits=[t_rs, t_sq])
                t_xn = t_xnh[1]
                prev_t_xn[slot] = t_xn
                for g in range(4):
                    tp_tok = None
                    for d8 in range(8):
                        dc = g * 8 + d8
                        tp_tok = P.op("pe", lambda E, dc=dc, d8=d8, sq_b=sq_b: E.transpose(
                            pbT[:, d8 * 128:(d8 + 1) * 128],
                            sq_b[:, dc * 128:(dc + 1) * 128], ident_b[:, :]),
                            PE if d8 == 7 else None,
                            waits=[t_xnh[g // 2], t_idb, ln_tp_tok[0], ln_tp_tok[1],
                                   passA_last])
                        if d8 == 7:
                            prev_tp[slot] = tp_tok
                    ln_tp_tok[g % 2] = P.op("act", lambda E, g=g, cs=cs: E.copy(
                        xn_span[:, g * 8:(g + 1) * 8, cs * 128:(cs + 1) * 128],
                        pbT[:, 0:1024].rearrange("p (a b) -> p a b", b=128)),
                        ACTS, waits=[tp_tok, xn_spill[sp - 1] if (sp > 0 and g == 0 and cs == 0) else None])
            tok_xn[sp] = (ACTS, ACTS.n)
            xn_spill[sp] = P.op("sp", DMA(xn_scr[sp, :, :],
                                          xn_span[:, :, :].rearrange("p a b -> p (a b)")),
                                QSP, 16, waits=[tok_xn[sp]])

            pa_w = [tok_xn[sp], t_wkr, t_at, t_rt, t_rot1, t_rot2,
                    ln_tp_tok[0], ln_tp_tok[1], t_logit_ev, t_res_ev, t_krope_dve,
                    g_dve_last, tok_cq_done]
            pa_w += [ckv_evac.get((sp - 1, rc)) for rc in range(4)]
            last = None
            grp_tok = {}
            for g in range(16):
                slot = g % 2
                wkv_rd[slot] = P.op("po", DMA(wkv_st[slot][:, :, :],
                                              wkv_bf[:, 2 * g:2 * g + 2, :]), QW[slot], 16,
                                    waits=[grp_tok.get(g - 2, (PE, PE.n) if g < 2 else None),
                                           t_wkvc if g < 2 else None])
                for d2 in range(2):
                    dc = 2 * g + d2
                    st = dc == 0
                    sp_ = dc == 31
                    w = pa_w + [wkv_rd[slot]] if d2 == 0 else [wkv_rd[slot]]
                    for rc in range(4):
                        last = P.op("pe", MM(pb[rc][:, :],
                                             wkv_st[slot][:, d2, rc * 128:(rc + 1) * 128],
                                             xn_span[:, dc, :], st, sp_), None, waits=w)
                        w = []
                    last = P.op("pe", MM(pb[4][0:64, :], wkr_sb[:, dc, :],
                                         xn_span[:, dc, :], st, sp_), None)
                    last = P.op("pe", MM(pb[5][0:64, :], wkr_rot[:, dc, :],
                                         xn_span[:, dc, :], st, sp_), None)
                    last = P.op("pe", MM(pb[6][0:64, :], at_sb[:, dc, :],
                                         xn_span[:, dc, :], st, sp_),
                                 PE if d2 == 1 else None)
                    if d2 == 1:
                        grp_tok[g] = last
            for dc in range(32):
                last = P.op("pe", MM(pb[6][64:128, :], at_sb[:, dc, :],
                                     xn_span[:, dc, :], dc == 0, dc == 31),
                            PE if dc == 31 else None)
            passA = last
            for rc in range(4):
                ckv_evac[(sp, rc)] = P.op("act", lambda E, rc=rc, sp=sp: E.copy(
                    ckv_fm[:, rc, sp * SPAN:(sp + 1) * SPAN], pb[rc][:, :]),
                    ACTS, waits=[passA])
            t1 = P.op("dve", lambda E, sp=sp: E.tensor_mul(
                rop1, pb[4][0:64, :], csk[:, sp * SPAN:(sp + 1) * SPAN]),
                DVE, waits=[passA, t_ktab])
            t2 = P.op("dve", lambda E, sp=sp: E.tensor_mul(
                rop2, pb[5][0:64, :], snk[:, sp * SPAN:(sp + 1) * SPAN]),
                DVE, waits=[passA])
            t_krope_dve = P.op("dve", lambda E, sp=sp: E.tensor_add(
                krope[:, sp * SPAN:(sp + 1) * SPAN], rop1, rop2),
                DVE, waits=[t1, t2])
            t_res_ev = P.op("dve", lambda E: E.tensor_copy(res2[:, :], pb[6][:, :]),
                            DVE, waits=[passA, (DVE, DVE.n)])
            lg_last = None
            for dc in range(32):
                lg_last = P.op("pe", MM(pb[4][0:16, :], rt_sb[:, dc, :],
                                        xn_span[:, dc, :], dc == 0, dc == 31),
                               PE if dc == 31 else None,
                               waits=[t_krope_dve] if dc == 0 else [])
            t_lgfm = P.op("dve", lambda E: E.tensor_copy(lg_fm[0:16, :], pb[4][0:16, :]),
                          DVE, waits=[lg_last])
            tp_lg = None
            for scq in range(4):
                tp_lg = P.op("pe", lambda E, s_=scq: E.transpose(
                    pb[5][:, s_ * 16:(s_ + 1) * 16], lg_fm[0:16, s_ * 128:(s_ + 1) * 128],
                    ident_f[0:16, 0:16]), PE if scq == 3 else None,
                    waits=[t_lgfm, t_id])
            t_logit_ev = P.op("dve", lambda E: E.tensor_copy(lg_sb[:, :], pb[5][:, 0:64]),
                              DVE, waits=[tp_lg])

            rt_tok = {}
            for scq in range(4):
                w0 = [t_logit_ev, (DVE, DVE.n), tp_comb_prev]
                t = P.op("dve", lambda E, s=scq: E.reduce_max(
                    small[:, 12:13], lg_sb[:, s * 16:(s + 1) * 16], axis=AX.X, negate=True),
                    DVE, waits=w0)
                t = P.op("act", lambda E, s=scq: E.activation(
                    e_sb[:, :], lg_sb[:, s * 16:(s + 1) * 16], AF.Exp, bias=small[:, 12:13]),
                    ACTS, waits=[t])
                t = P.op("dve", lambda E: E.reduce_sum(small[:, 13:14], e_sb[:, :], axis=AX.X),
                         DVE, waits=[t])
                t = P.op("dve", lambda E: E.reciprocal(small[:, 14:15], small[:, 13:14]),
                         DVE, waits=[t])
                t = P.op("dve", lambda E: E.tensor_scalar_mul(
                    q_sb[:, :], e_sb[:, :], small[:, 14:15]), DVE, waits=[t])
                t = P.op("dve", lambda E: E.reduce_max(small[:, 15:16], q_sb[:, :], axis=AX.X),
                         DVE, waits=[t])
                t = P.op("dve", lambda E: E.tensor_scalar(
                    ge1_sb[:, :], q_sb[:, :], small[:, 15:16], None, ALU.is_ge),
                    DVE, waits=[t])
                t = P.op("dve", lambda E: E.scalar_tensor_tensor(
                    qm_sb[:, :], ge1_sb[:, :], -1e30, q_sb[:, :], ALU.mult, ALU.add),
                    DVE, waits=[t])
                t = P.op("dve", lambda E: E.reduce_max(small[:, 16:17], qm_sb[:, :], axis=AX.X),
                         DVE, waits=[t])
                t = P.op("dve", lambda E: E.tensor_scalar(
                    ge2_sb[:, :], q_sb[:, :], small[:, 16:17], None, ALU.is_ge),
                    DVE, waits=[t])
                t = P.op("dve", lambda E: E.tensor_scalar(
                    small[:, 17:18], small[:, 16:17], small[:, 15:16], 1e-6, ALU.add, ALU.add),
                    DVE, waits=[t])
                t = P.op("dve", lambda E: E.reciprocal(small[:, 18:19], small[:, 17:18]),
                         DVE, waits=[t])
                t = P.op("dve", lambda E: E.tensor_scalar(
                    small[:, 19:20], small[:, 15:16], small[:, 18:19], 0.5, ALU.mult, ALU.mult),
                    DVE, waits=[t])
                t = P.op("dve", lambda E: E.tensor_scalar(
                    small[:, 20:21], small[:, 16:17], small[:, 18:19], 0.5, ALU.mult, ALU.mult),
                    DVE, waits=[t])
                t = P.op("dve", lambda E: E.tensor_sub(
                    small[:, 21:22], small[:, 19:20], small[:, 20:21]), DVE, waits=[t])
                t = P.op("dve", lambda E: E.tensor_scalar(
                    cmb_sb[:, :], ge2_sb[:, :], small[:, 20:21], None, ALU.mult),
                    DVE, waits=[t])
                t = P.op("dve", lambda E, s=scq: E.scalar_tensor_tensor(
                    cmb2_sb[:, s * 16:(s + 1) * 16], ge1_sb[:, :], small[:, 21:22],
                    cmb_sb[:, :], ALU.mult, ALU.add), DVE, waits=[t, tp_comb_prev])
                rt_tok[scq] = t
            for half in (0,):
                last = None
                grp_tok = {}
                for g in range(16):
                    slot = g % 2
                    wqd_rd[slot] = P.op("po", DMA(
                        wqd_st[slot][:, :, :], wqd_bf[:, 2 * g:2 * g + 2,
                                                      half * 768:(half + 1) * 768]),
                        QW[slot], 16,
                        waits=[grp_tok.get(g - 2, (PE, PE.n) if g < 2 else None),
                               t_wqdc if g < 2 else None])
                    for d2 in range(2):
                        dc = 2 * g + d2
                        st = dc == 0
                        sp_ = dc == 31
                        w = ([wqd_rd[slot], cq_ev, t_lgfm, t_logit_ev, (ACTS, ACTS.n)]
                             if (g == 0 and d2 == 0)
                             else ([wqd_rd[slot]] if d2 == 0 else []))
                        for q6 in range(6):
                            last = P.op("pe", MM(pb[q6][:, :],
                                                 wqd_st[slot][:, d2, q6 * 128:(q6 + 1) * 128],
                                                 xn_span[:, dc, :], st, sp_),
                                        PE if (sp_ or (d2 == 1 and q6 == 5)) else None,
                                        waits=w)
                            w = []
                        if d2 == 1 and q6 == 5:
                            grp_tok[g] = last
                for q6 in range(6):
                    qlc = half * 6 + q6
                    cq_ev = P.op("act", lambda E, qlc=qlc, q6=q6, sp=sp: E.copy(
                        c_q[:, qlc, sp * SPAN:(sp + 1) * SPAN], pb[q6][:, :]),
                        ACTS, waits=[last])
            tp_comb = None
            for scq in range(4):
                tp_comb = P.op("pe", lambda E, s=scq: E.transpose(
                    pb[6][0:16, s * 128:(s + 1) * 128],
                    cmb2_sb[:, s * 16:(s + 1) * 16], ident_f[:, :]),
                    PE if scq == 3 else None,
                    waits=[rt_tok[scq], t_id, t_res_ev])
            tp_comb_prev = tp_comb
            t_combT = P.op("dve", lambda E: E.tensor_copy(combT[0:16, :], pb[6][0:16, :]),
                           DVE, waits=[tp_comb])
            for c in range(8):
                tm1 = P.op("pe", MM(pb[6][0:64, :], ind_sb[:, (2 * c) * 64:(2 * c + 1) * 64],
                                    combT[0:16, :], True, True), None,
                           waits=[t_combT, t_ind, g_dve_last,
                                  g_spill[sp - 1] if (sp > 0 and c == 0) else None])
                tm2 = P.op("pe", MM(pb[6][64:128, :], ind_sb[:, (2 * c + 1) * 64:(2 * c + 2) * 64],
                                    combT[0:16, :], True, True), PE)
                g_dve_last = P.op("dve", lambda E, c=c: E.tensor_mul(
                    g_span[:, c, :], res2[:, :], pb[6][:, :]), DVE,
                    waits=[tm2, g_spill[sp - 1] if (sp > 0 and c == 0) else None])
            g_spill[sp] = P.op("sp", DMA(g_scr[sp, :, :],
                                         g_span[:, :, :].rearrange("p a b -> p (a b)")),
                               QSP, 16, waits=[g_dve_last])
            for half in (1,):
                last = None
                grp_tok = {}
                for g in range(16):
                    slot = g % 2
                    wqd_rd[slot] = P.op("po", DMA(
                        wqd_st[slot][:, :, :], wqd_bf[:, 2 * g:2 * g + 2,
                                                      half * 768:(half + 1) * 768]),
                        QW[slot], 16,
                        waits=[grp_tok.get(g - 2, (PE, PE.n) if g < 2 else None),
                               t_wqdc if g < 2 else None])
                    for d2 in range(2):
                        dc = 2 * g + d2
                        st = dc == 0
                        sp_ = dc == 31
                        w = ([wqd_rd[slot], cq_ev, t_lgfm, t_logit_ev, (ACTS, ACTS.n)]
                             if (g == 0 and d2 == 0)
                             else ([wqd_rd[slot]] if d2 == 0 else []))
                        for q6 in range(6):
                            last = P.op("pe", MM(pb[q6][:, :],
                                                 wqd_st[slot][:, d2, q6 * 128:(q6 + 1) * 128],
                                                 xn_span[:, dc, :], st, sp_),
                                        PE if (sp_ or (d2 == 1 and q6 == 5)) else None,
                                        waits=w)
                            w = []
                        if d2 == 1 and q6 == 5:
                            grp_tok[g] = last
                for q6 in range(6):
                    qlc = half * 6 + q6
                    cq_ev = P.op("act", lambda E, qlc=qlc, q6=q6, sp=sp: E.copy(
                        c_q[:, qlc, sp * SPAN:(sp + 1) * SPAN], pb[q6][:, :]),
                        ACTS, waits=[last])
            tok_cq_done = cq_ev
            passA_last = cq_ev

        P.barrier(ALLSEMS)
        tb2 = []
        for qi, (hi, lo, dstt) in enumerate(((csqh, csql, csq), (snqh, snql, snq))):
            t1 = P.op("sp", DMA(dstt[:, :], hi[:, :]), QT[qi], 16)
            t2 = P.op("sp", DMA(tlo2[:, :], lo[:, :]), QT[qi], 16, waits=tb2[-1:])
            tb2.append(P.op("dve", lambda E, d=dstt: E.tensor_add(d[:, :], d[:, :], tlo2[:, :]),
                            DVE, waits=[t1, t2] + tb2[-1:]))
        t_qtab = tb2[-1]
        t_mask = P.op("po", DMA(mask_sb[:, :, :],
                                maskd[:, :, :].rearrange("v p n -> p v n")), QCP, 16)
        tok_ckvtok = None
        ct_ev = {}
        for tc in range(16):
            for rc in range(4):
                bk = (tc * 4 + rc) % 2
                tt = P.op("pe", lambda E, tc=tc, rc=rc, bk=bk: E.transpose(
                    pbT[:, bk * 128:(bk + 1) * 128],
                    ckv_fm[:, rc, tc * 128:(tc + 1) * 128], ident_b[:, :]),
                    PE, waits=[ct_ev.get(0), ct_ev.get(1)])
                ct_ev[bk] = P.op("dve", lambda E, tc=tc, rc=rc, bk=bk: E.tensor_copy(
                    ckv_tok[:, tc, rc * 128:(rc + 1) * 128], pbT[:, bk * 128:(bk + 1) * 128]),
                    DVE, waits=[tt])
        tok_ckvtok = (DVE, DVE.n)
        if tap:
            P.op("sp", DMA(tap_cq[:, :, :], c_q[:, :, :]), QSP, 16, waits=[tok_cq_done])
            P.op("sp", DMA(tap_ckv[:, :, :], ckv_fm[:, :, :]), QSP, 16)
            P.op("sp", DMA(tap_ckvt[:, :, :], ckv_tok[:, :, :]), QSP, 16, waits=[tok_ckvtok])
            P.op("sp", DMA(tap_kro[:, :], krope[:, :]), QSP, 16)

        wq_rd = [None, None]
        wq_rot = [None, None]
        pe_slot_free = [None, None]
        am_dma = [None, None]

        def load_head(h):
            slot = h % 2
            w = [pe_slot_free[slot]]
            t1 = P.op("po", DMA(wqun_st[slot][:, :], wqun[h, :, :]), QW[slot], 16, waits=w)
            t2 = P.op("po", DMA(
                wqq_st[slot][:, :].rearrange("p (a b) -> p a b", b=128)[:, :, 0:64],
                wqur[h, :, :].rearrange("p (a b) -> p a b", b=DR)), QW[slot], 16)
            t3 = P.op("po", DMA(wuk_st[slot][:, :], wuk[h * 128:(h + 1) * 128, :]), QW[slot], 16)
            t4 = P.op("po", DMA(wuv_st[slot][:, :], wuvT[h, :, :]), QW[slot], 16)
            wq_rd[slot] = (t1, t2, t3, t4)
            r1 = P.op("dve", lambda E, sl=slot: E.tensor_scalar_mul(
                wqq_st[sl][:, :].rearrange("p (a b) -> p a b", b=128)[:, :, 64:96],
                wqq_st[sl][:, :].rearrange("p (a b) -> p a b", b=128)[:, :, 32:64], -1.0),
                DVE, waits=[t2])
            r2 = P.op("dve", lambda E, sl=slot: E.tensor_copy(
                wqq_st[sl][:, :].rearrange("p (a b) -> p a b", b=128)[:, :, 96:128],
                wqq_st[sl][:, :].rearrange("p (a b) -> p a b", b=128)[:, :, 0:32]),
                DVE, waits=[t2])
            wq_rot[slot] = (r1, r2)

        load_head(0)
        t_p3conv = None
        for h in range(H):
            slot = h % 2
            t1, t2, t3, t4 = wq_rd[slot]
            r1, r2 = wq_rot[slot]
            if h + 1 < H:
                load_head(h + 1)
            if h < 16:
                t_p3conv = P.op("po", DMA(bwT_bf[h, :, :], bwT[h, :, :]), QCP, 16)
                t_p3conv = P.op("po", DMA(wout_bf[h, :, :], wout[h, :, :]), QCP, 16)
                t_p3conv = P.op("po", DMA(bst_bf[h, :, :], bst[h, :, :]), QCP, 16)
            qn_ev = None
            qro_dve = None
            ql_ev = {}
            for sp in range(NSPAN):
                w0 = ([t1, tok_cq_done, qn_ev, (ACTS, ACTS.n), (DVE, DVE.n)]
                      if sp == 0 else [qn_ev])
                last = None
                for qlc in range(12):
                    last = P.op("pe", MM(pb[0][:, :], wqun_st[slot][:, qlc * 128:(qlc + 1) * 128],
                                         c_q[:, qlc, sp * SPAN:(sp + 1) * SPAN],
                                         qlc == 0, qlc == 11),
                                PE if qlc == 11 else None, waits=w0)
                    w0 = []
                qn_ev = P.op("act", lambda E, sp=sp: E.activation(
                    qn_sb[:, sp * SPAN:(sp + 1) * SPAN], pb[0][:, :], AF.Copy, scale=SCALE),
                    ACTS, waits=[last])
                w0 = [t2, r1, r2, qro_dve] + ([(ACTS, ACTS.n), (DVE, DVE.n)] if sp == 0 else [])
                for qlc in range(12):
                    last = P.op("pe", MM(pb[1][:, :], wqq_st[slot][:, qlc * 128:(qlc + 1) * 128],
                                         c_q[:, qlc, sp * SPAN:(sp + 1) * SPAN],
                                         qlc == 0, qlc == 11),
                                PE if qlc == 11 else None, waits=w0)
                    w0 = []
                ta = P.op("dve", lambda E, sp=sp: E.tensor_mul(
                    qt1[:, :], pb[1][0:64, :], csq[:, sp * SPAN:(sp + 1) * SPAN]),
                    DVE, waits=[last, t_qtab])
                tb = P.op("dve", lambda E, sp=sp: E.tensor_mul(
                    qt2[:, :], pb[1][64:128, :], snq[:, sp * SPAN:(sp + 1) * SPAN]),
                    DVE, waits=[last])
                qro_dve = P.op("dve", lambda E, sp=sp: E.tensor_add(
                    qro_sb[:, sp * SPAN:(sp + 1) * SPAN], qt1[:, :], qt2[:, :]),
                    DVE, waits=[ta, tb])
            for rc in range(4):
                for sp in range(NSPAN):
                    bk = 3 + (rc * NSPAN + sp) % 4
                    tm = P.op("pe", MM(pb[bk][:, :], wuk_st[slot][:, rc * 128:(rc + 1) * 128],
                                       qn_sb[:, sp * SPAN:(sp + 1) * SPAN], True, True),
                              PE, waits=[t3, qn_ev, ql_ev.get(bk), (DVE, DVE.n)])
                    ql_ev[bk] = P.op("act", lambda E, rc=rc, sp=sp, bk=bk: E.copy(
                        qlat[:, rc, sp * SPAN:(sp + 1) * SPAN], pb[bk][:, :]),
                        ACTS, waits=[tm])
            tok_qsetup = (ACTS, ACTS.n)

            exp_toks = {}
            s_recip_tok = None
            ol_ev = {3: ql_ev.get(3), 4: ql_ev.get(4)}
            sc_ev = {0: None, 1: None}
            bc_ev = None
            am_mul = None
            for qs in range(NSPAN):
                ntc = 4 * qs + 4
                exp_list = []
                for tc in range(ntc):
                    bk = tc % 2
                    w = [tok_qsetup, qro_dve, tok_ckvtok, exp_toks.get(bk), t_mask,
                         ql_ev.get(5), ql_ev.get(6), sc_ev[bk]]
                    last = None
                    for rc in range(4):
                        last = P.op("pe", MM(pb[bk][:, :],
                                             ckv_fm[:, rc, tc * 128:(tc + 1) * 128],
                                             qlat[:, rc, qs * SPAN:(qs + 1) * SPAN],
                                             rc == 0, False), None, waits=w)
                        w = []
                    last = P.op("pe", MM(pb[bk][:, :], krope[:, tc * 128:(tc + 1) * 128],
                                         qro_sb[:, qs * SPAN:(qs + 1) * SPAN],
                                         False, True), PE)
                    sc_ev[bk] = last
                    te = P.op("act", lambda E, tc=tc, bk=bk: E.activation(
                        expT[:, tc, :], pb[bk][:, :], AF.Exp), ACTS, waits=[last])
                    if tc >= 4 * qs:
                        te = P.op("dve", lambda E, tc=tc, v=tc - 4 * qs: E.tensor_mul(
                            expT[:, tc, :], expT[:, tc, :], mask_sb[:, v, :]),
                            DVE, waits=[te])
                    exp_toks[bk] = te
                    exp_list.append(te)
                s_last = None
                for tc in range(ntc):
                    s_last = P.op("pe", MM(pb[6][0:1, :], ones_col[:, 0:1],
                                           expT[:, tc, :], tc == 0, tc == ntc - 1),
                                  PE if tc == ntc - 1 else None,
                                  waits=[exp_list[tc], s_recip_tok])
                s_recip_tok = P.op("dve", lambda E: E.reciprocal(
                    rs_row[0:1, :], pb[6][0:1, :]), DVE, waits=[s_last])
                tbc = None
                for rc in range(4):
                    bk = 3 + rc % 2
                    w = [ol_ev.get(bk)]
                    last = None
                    for tc in range(ntc):
                        last = P.op("pe", MM(pb[bk][:, :],
                                             ckv_tok[:, tc, rc * 128:(rc + 1) * 128],
                                             expT[:, tc, :], tc == 0, tc == ntc - 1),
                                      PE if tc == ntc - 1 else None, waits=w)
                        w = []
                    ol_ev[bk] = P.op("act", lambda E, rc=rc, bk=bk: E.copy(
                        ol_sb[:, rc, :], pb[bk][:, :]), ACTS, waits=[last])
                    if rc == 0:
                        tbc = P.op("pe", MM(pb[2][:, :], ones_l1f[0:1, :], rs_row[0:1, :],
                                            True, True), PE, waits=[s_recip_tok, am_mul])
                bc_ev = P.op("dve", lambda E: E.tensor_copy(rsbc[:, :], pb[2][:, :]),
                             DVE, waits=[tbc])
                w = [ol_ev.get(3), ol_ev.get(4), t4, am_mul]
                last = None
                for rc in range(4):
                    last = P.op("pe", MM(pb[5][:, :], wuv_st[slot][:, rc * 128:(rc + 1) * 128],
                                         ol_sb[:, rc, :], rc == 0, rc == 3),
                                PE if rc == 3 else None, waits=w)
                    w = []
                am_slot = qs % 2
                am_mul = P.op("dve", lambda E, sl=am_slot: E.tensor_mul(
                    am_tile[sl][:, :], pb[5][:, :], rsbc[:, :]),
                    DVE, waits=[last, bc_ev, am_dma[am_slot]])
                am_dma[am_slot] = P.op("sp", DMA(
                    am_scr[qs, :, h * SPAN:(h + 1) * SPAN], am_tile[am_slot][:, :]),
                    QX[am_slot], 16, waits=[am_mul])
            pe_slot_free[slot] = (PE, PE.n)

        P.barrier(ALLSEMS)
        t_bb = P.op("po", DMA(bb_sb[0:1, :], bb[:, :]), QCP, 16)
        w_rd = [None, None]
        pe_w_free = [None, None]
        ep_dve = [None] * 4
        out_dma = [None] * 4
        xr_dma = [None] * 4

        def load_w(qt, osp):
            slot = osp % 2
            w = [(PE, PE.n)]
            a = P.op("po", DMA(bwT_st[slot][:, :, :],
                               bwT_bf[osp, :, :].rearrange("p (a b) -> p a b", b=OSP)),
                     QW[slot], 16, waits=w + [t_p3conv])
            b = P.op("po", DMA(wout_st[slot][:, :, :],
                               wout_bf[osp, :, :].rearrange("p (a b) -> p a b", b=OSP)), QW[slot], 16)
            c = P.op("po", DMA(bst_st[slot][:, :, :],
                               bst_bf[osp, :, :].rearrange("p (a b) -> p a b", b=OSP)), QW[slot], 16)
            w_rd[slot] = (a, b, c)

        for qt in range(NSPAN):
            ta = P.op("sp", DMA(xnq_sb[:, :, :].rearrange("p a b -> p (a b)"),
                                xn_scr[qt, :, :]), QSP, 16,
                      waits=[(PE, PE.n)])
            tb_ = P.op("po", DMA(amq_sb[:, :, :].rearrange("p a b -> p (a b)"),
                                 am_scr[qt, :, :]), QCP, 16, waits=[(PE, PE.n)])
            tg = P.op("po", DMA(gq_sb[:, :, :].rearrange("p a b -> p (a b)"),
                                g_scr[qt, :, :]), QCP, 16)
            if qt == 0:
                load_w(0, 0)
            acts_tok = [ta, tb_, tg]
            for osp in range(NOSP):
                slot = osp % 2
                a, b, c = w_rd[slot]
                if osp + 1 < NOSP:
                    load_w(qt, osp + 1)
                elif qt + 1 < NSPAN:
                    load_w(qt + 1, 0)
                for scq in range(4):
                    bk = scq
                    r0 = qt * SPAN + scq * 128
                    w = acts_tok + [a, b, c, t_bb, ep_dve[bk]]
                    last = None
                    for ic in range(32):
                        last = P.op("pe", MM(pb[bk][:, 0:OSP],
                                             xnq_sb[:, ic, scq * 128:(scq + 1) * 128],
                                             bwT_st[slot][:, ic, :], ic == 0, False),
                                    None, waits=w)
                        w = []
                    for hh in range(32):
                        last = P.op("pe", MM(pb[bk][:, 0:OSP],
                                             amq_sb[:, hh, scq * 128:(scq + 1) * 128],
                                             wout_st[slot][:, hh, :], False, False), None)
                    for cc in range(8):
                        last = P.op("pe", MM(pb[bk][:, 0:OSP],
                                             gq_sb[:, cc, scq * 128:(scq + 1) * 128],
                                             bst_st[slot][:, cc, :], False, False), None)
                    last = P.op("pe", MM(pb[bk][:, 0:OSP], ones_l1b[0:1, :],
                                         bb_sb[0:1, osp * OSP:(osp + 1) * OSP],
                                         False, True), PE)
                    if osp == NOSP - 1:
                        pe_w_free[slot] = (PE, PE.n)
                    xr_dma[bk] = P.op("sp", DMA(xres[bk][:, :],
                                                x[r0:r0 + 128, osp * OSP:(osp + 1) * OSP]),
                                      QX[bk % 2], 16,
                                      waits=[ep_dve[bk], ep_dve[(bk + 2) % 4]])
                    ep_dve[bk] = P.op("dve", lambda E, bk=bk: E.tensor_add(
                        out_sb[bk][:, :], xres[bk][:, :], pb[bk][:, 0:OSP]),
                        DVE, waits=[last, xr_dma[bk], out_dma[bk]])
                    out_dma[bk] = P.op("sp", DMA(
                        out[r0:r0 + 128, osp * OSP:(osp + 1) * OSP], out_sb[bk][:, :]),
                        QX[bk % 2], 16, waits=[ep_dve[bk]])
                if osp == NOSP - 1:
                    pe_w_free[slot] = (PE, PE.n)

        P.op("sp", lambda E: E.wait_ge(QX[0].h, QX[0].n))
        P.op("sp", lambda E: E.wait_ge(QX[1].h, QX[1].n))
        P.op("sp", lambda E: E.wait_ge(QSP.h, QSP.n))

        with nc.Block() as block:

            @block.tensor
            def _(E):
                for f in P.q["pe"]:
                    f(E)

            @block.scalar
            def _(E):
                for f in P.q["act"]:
                    f(E)

            @block.vector
            def _(E):
                for f in P.q["dve"]:
                    f(E)

            @block.gpsimd
            def _(E):
                for f in P.q["po"]:
                    f(E)

            @block.sync
            def _(E):
                for f in P.q["sp"]:
                    f(E)

    return nc


def _aux_inputs():
    f64 = np.float64
    f32 = np.float32
    freqs = (1.0 / 10000.0 ** (np.arange(0, DR, 2, dtype=f32) / f32(DR))).astype(f32)
    wavelengths = (f32(2.0 * np.pi) / freqs).astype(f32)
    ramp = np.clip((wavelengths / f32(4096.0) - f32(1.0)) / f32(31.0),
                   0.0, 1.0).astype(f32)
    inv_freq = (freqs / (f32(1.0) - ramp + ramp * f32(40.0))).astype(f32)
    pos = np.arange(S, dtype=f32)
    ang = (pos[None, :] * inv_freq[:, None]).astype(f32).astype(f64)
    cos2 = np.concatenate([np.cos(ang)] * 2, 0)
    sin2 = np.concatenate([np.sin(ang)] * 2, 0)
    sc = f64(1.0) / np.sqrt(f64(192.0))

    def hilo(t):
        hi = t.astype(np.float32)
        lo = (t - hi.astype(f64)).astype(np.float32)
        return hi, lo

    csqh, csql = hilo(cos2 * sc)
    snqh, snql = hilo(sin2 * sc)
    cskh, cskl = hilo(cos2)
    snkh, snkl = hilo(sin2)
    mask = np.zeros((4, 128, SPAN), np.float32)
    j = np.arange(128)[:, None]
    i = np.arange(SPAN)[None, :]
    for v in range(4):
        mask[v] = (j <= i - 128 * v).astype(np.float32)
    ind = np.zeros((NE, NE * NR), np.float32)
    for e in range(NE):
        ind[e, e * NR:(e + 1) * NR] = 1.0
    return dict(ident=np.eye(128, dtype=np.float32),
                csqh=csqh, csql=csql, snqh=snqh, snql=snql,
                cskh=cskh, cskl=cskl, snkh=snkh, snkl=snkl,
                maskd=mask, indd=ind)


def prepare_weight_maps(L0_attn_w_q_down, L0_attn_w_q_up_nope, L0_attn_w_q_up_rope,
                        L0_attn_w_kv_down, L0_attn_w_k_rope, L0_attn_w_uk,
                        L0_attn_w_uv, L0_attn_w_out, L0_ffn_base_w, L0_ffn_base_b,
                        L0_ffn_A, L0_ffn_B, L0_ffn_router_w):
    c = np.ascontiguousarray
    m = {}
    m["wqd"] = c(L0_attn_w_q_down)
    m["wqun"] = c(L0_attn_w_q_up_nope.reshape(12, 128, H, DN).transpose(2, 1, 0, 3)
                  .reshape(H, 128, QL))
    m["wqur"] = c(L0_attn_w_q_up_rope.reshape(12, 128, H, DR).transpose(2, 1, 0, 3)
                  .reshape(H, 128, 12 * DR))
    m["wkv"] = c(L0_attn_w_kv_down)
    m["wkr"] = c(L0_attn_w_k_rope)
    m["wuk"] = c(L0_attn_w_uk)
    m["wuvT"] = c(L0_attn_w_uv.reshape(H, DV, 4, 128).transpose(0, 3, 2, 1)
                  .reshape(H, 128, 4 * DV))
    m["wout"] = c(L0_attn_w_out.reshape(32, 128, NOSP, OSP).transpose(2, 1, 0, 3)
                  .reshape(NOSP, 128, 32 * OSP))
    m["bwT"] = c(L0_ffn_base_w.T.reshape(32, 128, NOSP, OSP).transpose(2, 1, 0, 3)
                 .reshape(NOSP, 128, 32 * OSP))
    m["bst"] = c(L0_ffn_B.transpose(0, 2, 1).reshape(8, 128, NOSP, OSP)
                 .transpose(2, 1, 0, 3).reshape(NOSP, 128, 8 * OSP))
    m["bb"] = c(L0_ffn_base_b.reshape(1, D))
    m["at_"] = c(L0_ffn_A.T)
    m["rt"] = c(L0_ffn_router_w.T)
    return m


def kernel(L0_attn_hidden_states, L0_attn_w_q_down, L0_attn_w_q_up_nope,
           L0_attn_w_q_up_rope, L0_attn_w_kv_down, L0_attn_w_k_rope, L0_attn_w_uk,
           L0_attn_w_uv, L0_attn_w_out, L0_ffn_base_w, L0_ffn_base_b, L0_ffn_A,
           L0_ffn_B, L0_ffn_router_w):
    hs = np.ascontiguousarray(L0_attn_hidden_states, dtype=np.float32)
    wm = prepare_weight_maps(L0_attn_w_q_down, L0_attn_w_q_up_nope, L0_attn_w_q_up_rope,
                             L0_attn_w_kv_down, L0_attn_w_k_rope, L0_attn_w_uk,
                             L0_attn_w_uv, L0_attn_w_out, L0_ffn_base_w, L0_ffn_base_b,
                             L0_ffn_A, L0_ffn_B, L0_ffn_router_w)
    wm.update(_aux_inputs())
    nc = build_nc()
    in_maps = []
    for cix in range(NCORES):
        im = dict(wm)
        im["x"] = np.ascontiguousarray(hs[cix])
        in_maps.append(im)
    res = run_bass_kernel_spmd(nc, in_maps, core_ids=list(range(NCORES)))
    return np.stack([r["out"] for r in res.results], axis=0)


# version 17
# speedup vs baseline: 1.0099x; 1.0099x over previous
import math
from contextlib import ExitStack

import numpy as np

import concourse.bass as bass
import concourse.mybir as mybir
from concourse.bass_utils import run_bass_kernel_spmd

F32 = mybir.dt.float32
BF16 = mybir.dt.bfloat16
AF = mybir.ActivationFunctionType
ALU = mybir.AluOpType
AX = mybir.AxisListType

S, D, H, QL, KVL, DN, DR, DV, NE, NR = 2048, 4096, 32, 1536, 512, 128, 64, 128, 16, 64
SPAN, NSPAN, CH = 512, 4, 128
OSP, NOSP = 256, 16
SCALE = 1.0 / math.sqrt(192.0)
EPS = 1e-5
NCORES = 8
ENGS = ("pe", "act", "dve", "po", "sp")


def _round32(x):
    return (x + 31) & ~31


class _Sem:
    def __init__(self, es, nc, name, is_queue=False):
        self.h = es.enter_context(nc.semaphore(name))
        self.name = name
        self.n = 0
        self.is_queue = is_queue

    def bump(self, k):
        self.n += k
        return self.n


class _Ovl:

    def __init__(self, nc, total):
        self.nc = nc
        self.total = total
        self.off = 0
        self.stack = []
        self.tensors = []

    def alloc(self, name, shape, dtype):
        per = int(np.prod(shape[1:])) * mybir.dt.size(dtype)
        off = _round32(self.off)
        self.off = off + per
        assert self.off <= self.total, (name, self.off, self.total)
        h = self.nc.virtual_sbuf_tensor(name, list(shape), dtype)
        self.tensors.append((h, off))
        return h

    def push(self):
        self.stack.append(self.off)

    def pop(self):
        self.off = self.stack.pop()

    def finalize(self, es):
        backing = es.enter_context(
            self.nc.sbuf_tensor("ovl_backing", [128, self.total], mybir.dt.uint8))
        base = self.nc.lookup_mloc(backing).addr
        for h, off in self.tensors:
            self.nc._tensor(h.name, h.shape, h.dtype, type="SB")
            mloc = self.nc.lookup_mloc(h)
            mloc.allocated = not self.nc.target_bir_lowering
            mloc.addr = base + off


class _Prog:

    def __init__(self):
        self.q = {e: [] for e in ENGS}
        self.wm = {}

    def wait(self, eng, tok):
        if tok is None:
            return
        sem, v = tok
        if getattr(sem, "is_queue", False):
            v = sem.n
        if self.wm.get((eng, sem.name), 0) >= v:
            return
        self.wm[(eng, sem.name)] = v
        self.q[eng].append(lambda E, h=sem.h, vv=v: E.wait_ge(h, vv))

    def op(self, eng, build, sem=None, k=1, waits=()):
        for t in waits:
            self.wait(eng, t)
        if sem is None:
            self.q[eng].append(build)
            return None
        v = sem.bump(k)
        self.q[eng].append(lambda E, b=build, h=sem.h, kk=k: b(E).then_inc(h, kk))
        return (sem, v)

    def barrier(self, sems):
        toks = [(s, s.n) for s in sems]
        for e in ENGS:
            for t in toks:
                self.wait(e, t)


def build_nc(tap=None):
    nc = bass.Bass("TRN2", target_bir_lowering=False)

    x = nc.dram_tensor("x", [S, D], F32, kind="ExternalInput")
    wqd = nc.dram_tensor("wqd", [D, QL], F32, kind="ExternalInput")
    wqun = nc.dram_tensor("wqun", [H, 128, QL], F32, kind="ExternalInput")
    wqur = nc.dram_tensor("wqur", [H, 128, 12 * DR], F32, kind="ExternalInput")
    wkv = nc.dram_tensor("wkv", [D, KVL], F32, kind="ExternalInput")
    wkr = nc.dram_tensor("wkr", [D, DR], F32, kind="ExternalInput")
    wuk = nc.dram_tensor("wuk", [H * DN, KVL], F32, kind="ExternalInput")
    wuvT = nc.dram_tensor("wuvT", [H, 128, 4 * DV], F32, kind="ExternalInput")
    wout = nc.dram_tensor("wout", [NOSP, 128, 32 * OSP], F32, kind="ExternalInput")
    bwT = nc.dram_tensor("bwT", [NOSP, 128, 32 * OSP], F32, kind="ExternalInput")
    bst = nc.dram_tensor("bst", [NOSP, 128, 8 * OSP], F32, kind="ExternalInput")
    bb = nc.dram_tensor("bb", [1, D], F32, kind="ExternalInput")
    at_ = nc.dram_tensor("at_", [D, NR], F32, kind="ExternalInput")
    rt = nc.dram_tensor("rt", [D, NE], F32, kind="ExternalInput")
    ident = nc.dram_tensor("ident", [128, 128], F32, kind="ExternalInput")
    csqh = nc.dram_tensor("csqh", [DR, S], F32, kind="ExternalInput")
    csql = nc.dram_tensor("csql", [DR, S], F32, kind="ExternalInput")
    snqh = nc.dram_tensor("snqh", [DR, S], F32, kind="ExternalInput")
    snql = nc.dram_tensor("snql", [DR, S], F32, kind="ExternalInput")
    cskh = nc.dram_tensor("cskh", [DR, S], F32, kind="ExternalInput")
    cskl = nc.dram_tensor("cskl", [DR, S], F32, kind="ExternalInput")
    snkh = nc.dram_tensor("snkh", [DR, S], F32, kind="ExternalInput")
    snkl = nc.dram_tensor("snkl", [DR, S], F32, kind="ExternalInput")
    maskd = nc.dram_tensor("maskd", [4, 128, SPAN], F32, kind="ExternalInput")
    indd = nc.dram_tensor("indd", [NE, NE * NR], F32, kind="ExternalInput")

    scr_kind = "ExternalOutput" if tap else "Internal"
    wqd_bf = nc.dram_tensor("wqd_bf", [128, 32, QL], BF16, kind="Internal")
    wkv_bf = nc.dram_tensor("wkv_bf", [128, 32, KVL], BF16, kind="Internal")
    bwT_bf = nc.dram_tensor("bwT_bf", [NOSP, 128, 32 * OSP], BF16, kind="Internal")
    wout_bf = nc.dram_tensor("wout_bf", [NOSP, 128, 32 * OSP], BF16, kind="Internal")
    bst_bf = nc.dram_tensor("bst_bf", [NOSP, 128, 8 * OSP], BF16, kind="Internal")
    xn_scr = nc.dram_tensor("xn_scr", [NSPAN, 128, 32 * SPAN], BF16, kind=scr_kind)
    am_scr = nc.dram_tensor("am_scr", [NSPAN, 128, 32 * SPAN], BF16, kind=scr_kind)
    g_scr = nc.dram_tensor("g_scr", [NSPAN, 128, 8 * SPAN], BF16, kind=scr_kind)

    out = nc.dram_tensor("out", [S, D], F32, kind="ExternalOutput")
    if tap:
        tap_cq = nc.dram_tensor("tap_cq", [128, 12, S], BF16, kind="ExternalOutput")
        tap_ckv = nc.dram_tensor("tap_ckv", [128, 4, S], BF16, kind="ExternalOutput")
        tap_ckvt = nc.dram_tensor("tap_ckvt", [128, 16, KVL], BF16, kind="ExternalOutput")
        tap_kro = nc.dram_tensor("tap_kro", [DR, S], BF16, kind="ExternalOutput")
        tap_q = nc.dram_tensor("tap_q", [128, 16], F32, kind="ExternalOutput")
        tap_sm = nc.dram_tensor("tap_sm", [128, 24], F32, kind="ExternalOutput")
        tap_ge1 = nc.dram_tensor("tap_ge1", [128, 16], F32, kind="ExternalOutput")
        tap_ge2 = nc.dram_tensor("tap_ge2", [128, 16], F32, kind="ExternalOutput")
        tap_cmb = nc.dram_tensor("tap_cmb", [128, 16], F32, kind="ExternalOutput")
        tap_lg = nc.dram_tensor("tap_lg", [128, 64], F32, kind="ExternalOutput")
        tap_qn = nc.dram_tensor("tap_qn", [128, S], BF16, kind="ExternalOutput")
        tap_qro = nc.dram_tensor("tap_qro", [DR, S], BF16, kind="ExternalOutput")
        tap_qlat = nc.dram_tensor("tap_qlat", [128, 4, S], BF16, kind="ExternalOutput")
        tap_expT = nc.dram_tensor("tap_expT", [128, 16, SPAN], BF16, kind="ExternalOutput")
        tap_rs = nc.dram_tensor("tap_rs", [1, SPAN], F32, kind="ExternalOutput")
        tap_ol = nc.dram_tensor("tap_ol", [128, 4, SPAN], BF16, kind="ExternalOutput")

    es = ExitStack()
    with es:
        pb = [es.enter_context(nc.psum_tensor(f"pb{i}", [128, 512], F32))
              for i in range(7)]
        pbT = es.enter_context(nc.psum_tensor("pbT", [128, 1024], BF16))

        ovl = _Ovl(nc, 212480)
        ident_f = ovl.alloc("ident_f", [128, 128], F32)
        ident_b = ovl.alloc("ident_b", [128, 128], BF16)
        ones_col = ovl.alloc("ones_col", [128, 1], BF16)
        ones_l1f = ovl.alloc("ones_l1f", [1, 128], F32)
        ones_l1b = ovl.alloc("ones_l1b", [1, 128], BF16)
        eps_sb = ovl.alloc("eps_sb", [128, 1], F32)
        ovl.push()
        c_q = ovl.alloc("c_q", [128, 12, S], BF16)
        ckv_fm = ovl.alloc("ckv_fm", [128, 4, S], BF16)
        krope = ovl.alloc("krope", [DR, S], BF16)
        ovl.push()
        xn_span = ovl.alloc("xn_span", [128, 32, SPAN], BF16)
        xch = [ovl.alloc(f"xch{i}", [128, D], F32) for i in range(2)]
        sqxn2 = [ovl.alloc(f"sqxn{i}", [128, D], BF16) for i in range(2)]
        wqd_st = [ovl.alloc(f"wqd_st{i}", [128, 2, 768], BF16) for i in range(2)]
        wkv_st = [ovl.alloc(f"wkv_st{i}", [128, 2, KVL], BF16) for i in range(2)]
        wkr_sb = ovl.alloc("wkr_sb", [128, 32, DR], BF16)
        wkr_rot = ovl.alloc("wkr_rot", [128, 32, DR], BF16)
        at_sb = ovl.alloc("at_sb", [128, 32, NR], BF16)
        rt_sb = ovl.alloc("rt_sb", [128, 32, NE], BF16)
        csk = ovl.alloc("csk", [DR, S], F32)
        snk = ovl.alloc("snk", [DR, S], F32)
        ind_sb = ovl.alloc("ind_sb", [NE, NE * NR], F32)
        res2 = ovl.alloc("res2", [128, SPAN], F32)
        g_span = ovl.alloc("g_span", [128, 8, SPAN], BF16)
        lg_sb = ovl.alloc("lg_sb", [128, 64], F32)
        lg_fm = ovl.alloc("lg_fm", [16, SPAN], F32)
        e_sb = ovl.alloc("e_sb", [128, 16], F32)
        q_sb = ovl.alloc("q_sb", [128, 16], F32)
        ge1_sb = ovl.alloc("ge1_sb", [128, 16], F32)
        qm_sb = e_sb
        ge2_sb = ovl.alloc("ge2_sb", [128, 16], F32)
        cmb_sb = ovl.alloc("cmb_sb", [128, 16], F32)
        cmb2_sb = ovl.alloc("cmb2_sb", [128, 64], F32)
        combT = ovl.alloc("combT", [NE, SPAN], F32)
        small = ovl.alloc("small", [128, 24], F32)
        p1_high = ovl.off
        ovl.pop()
        ovl.push()
        ckv_tok = ovl.alloc("ckv_tok", [128, 16, KVL], BF16)
        tlo2 = ovl.alloc("tlo2", [DR, S], F32)
        csq = ovl.alloc("csq", [DR, S], F32)
        snq = ovl.alloc("snq", [DR, S], F32)
        mask_sb = ovl.alloc("mask_sb", [128, 4, SPAN], BF16)
        wqun_st = [ovl.alloc(f"wqun_st{i}", [128, QL], BF16) for i in range(2)]
        wqq_st = [ovl.alloc(f"wqq_st{i}", [128, 12 * 128], BF16) for i in range(2)]
        wuk_st = [ovl.alloc(f"wuk_st{i}", [128, KVL], BF16) for i in range(2)]
        wuv_st = [ovl.alloc(f"wuv_st{i}", [128, 4 * DV], BF16) for i in range(2)]
        qn_sb = ovl.alloc("qn_sb", [128, S], BF16)
        qro_sb = ovl.alloc("qro_sb", [DR, S], BF16)
        qlat = ovl.alloc("qlat", [128, 4, S], BF16)
        expT = ovl.alloc("expT", [128, 16, SPAN], BF16)
        ol_sb = ovl.alloc("ol_sb", [128, 4, SPAN], BF16)
        rs_tok = ovl.alloc("rs_tok", [128, 4], F32)
        rs_row = ovl.alloc("rs_row", [1, SPAN], F32)
        rsbc = ovl.alloc("rsbc", [128, SPAN], F32)
        am_tile = [ovl.alloc(f"am_tile{i}", [128, SPAN], BF16) for i in range(2)]
        qt1 = ovl.alloc("qt1", [DR, SPAN], F32)
        qt2 = ovl.alloc("qt2", [DR, SPAN], F32)
        p2_high = ovl.off
        ovl.pop()
        ovl.pop()
        xnq_sb = ovl.alloc("xnq_sb", [128, 32, SPAN], BF16)
        amq_sb = ovl.alloc("amq_sb", [128, 32, SPAN], BF16)
        gq_sb = ovl.alloc("gq_sb", [128, 8, SPAN], BF16)
        bwT_st = [ovl.alloc(f"bwT_st{i}", [128, 32, OSP], BF16) for i in range(2)]
        wout_st = [ovl.alloc(f"wout_st{i}", [128, 32, OSP], BF16) for i in range(2)]
        bst_st = [ovl.alloc(f"bst_st{i}", [128, 8, OSP], BF16) for i in range(2)]
        bb_sb = ovl.alloc("bb_sb", [1, D], BF16)
        xres = [ovl.alloc(f"xres{i}", [128, OSP], F32) for i in range(4)]
        out_sb = [ovl.alloc(f"out_sb{i}", [128, OSP], F32) for i in range(4)]
        p3_high = ovl.off
        ovl.finalize(es)

        PE = _Sem(es, nc, "s_pe")
        ACTS = _Sem(es, nc, "s_act")
        DVE = _Sem(es, nc, "s_dve")
        QSP = _Sem(es, nc, "q_sp", is_queue=True)
        QC = _Sem(es, nc, "q_c", is_queue=True)
        QCP = _Sem(es, nc, "q_cp", is_queue=True)
        QT = [_Sem(es, nc, f"q_t{i}", is_queue=True) for i in range(2)]
        QW = [_Sem(es, nc, f"q_w{i}", is_queue=True) for i in range(2)]
        QX = [_Sem(es, nc, f"q_x{i}", is_queue=True) for i in range(2)]
        ALLSEMS = [PE, ACTS, DVE, QSP, QC, QCP] + QT + QW + QX

        P = _Prog()

        def MM(outp, lhsT, rhs, start, stop):
            return lambda E: E.matmul(outp, lhsT, rhs, start=start, stop=stop)

        def DMA(dst, src):
            return lambda E: E.dma_start(out=dst, in_=src)

        t_id = P.op("sp", DMA(ident_f[:, :], ident[:, :]), QC, 16)
        t_idb = P.op("po", DMA(ident_b[:, :], ident[:, :]), QCP, 16)
        P.op("dve", lambda E: E.memset(ones_col[:, :], 1.0), DVE)
        P.op("dve", lambda E: E.memset(ones_l1f[0:1, :], 1.0), DVE)
        P.op("dve", lambda E: E.memset(ones_l1b[0:1, :], 1.0), DVE)
        t_ones = P.op("dve", lambda E: E.memset(eps_sb[:, :], EPS), DVE)
        tlo = xch[0]
        rop1 = xch[0][0:64, 1024:1536]
        rop2 = xch[0][0:64, 1536:2048]
        tb_toks = []
        for qi, (hi, lo, dstt) in enumerate(((cskh, cskl, csk), (snkh, snkl, snk))):
            t1 = P.op("sp", DMA(dstt[:, :], hi[:, :]), QT[qi], 16)
            t2 = P.op("sp", DMA(tlo[0:DR, 0:S], lo[:, :]), QT[qi], 16,
                      waits=tb_toks[-1:])
            tb_toks.append(P.op("dve", lambda E, d=dstt: E.tensor_add(d[:, :], d[:, :], tlo[0:DR, 0:S]),
                                DVE, waits=[t1, t2] + tb_toks[-1:]))
        t_ktab = tb_toks[-1]
        wkr_r = wkr[:, :].rearrange("(g p) n -> p g n", p=128)
        at_r = at_[:, :].rearrange("(g p) n -> p g n", p=128)
        rt_r = rt[:, :].rearrange("(g p) n -> p g n", p=128)
        t_wkr = P.op("po", DMA(wkr_sb[:, :, :], wkr_r), QCP, 16)
        t_at = P.op("po", DMA(at_sb[:, :, :], at_r), QCP, 16)
        t_rt = P.op("po", DMA(rt_sb[:, :, :], rt_r), QCP, 16)
        t_ind = P.op("sp", DMA(ind_sb[:, :], indd[:, :]), QC, 16)
        wqd_r = wqd[:, :].rearrange("(g p) n -> p g n", p=128)
        wkv_r = wkv[:, :].rearrange("(g p) n -> p g n", p=128)
        t_wqdc = P.op("po", DMA(wqd_bf[:, :, :], wqd_r), QCP, 16)
        t_wkvc = P.op("po", DMA(wkv_bf[:, :, :], wkv_r), QCP, 16)
        t_rot1 = P.op("dve", lambda E: E.tensor_scalar_mul(
            wkr_rot[:, :, 0:32], wkr_sb[:, :, 32:64], -1.0), DVE, waits=[t_wkvc])
        t_rot2 = P.op("dve", lambda E: E.tensor_copy(
            wkr_rot[:, :, 32:64], wkr_sb[:, :, 0:32]), DVE, waits=[t_wkvc])

        tok_xn = [None] * NSPAN
        tok_cq_done = None
        ln_tp_tok = [None, None]
        ckv_evac = {}
        wkv_rd = [None, None]
        wqd_rd = [None, None]
        t_logit_ev = None
        t_res_ev = None
        g_spill = [None] * NSPAN
        xn_spill = [None] * NSPAN
        t_krope_dve = None
        tp_comb_prev = None
        g_dve_last = None
        tok_cq_done = None
        cq_ev = None
        passA_last = None
        prev_tp = [None, None]
        prev_t_xn = [None, None]
        prev_t_sq = [None, None]

        for sp in range(NSPAN):
            for cs in range(4):
                tc16 = sp * 4 + cs
                slot = tc16 % 2
                c0 = slot * 6
                r0 = tc16 * CH
                sq_b = sqxn2[slot]
                t_x = P.op("sp", DMA(xch[slot][:, :], x[r0:r0 + CH, :]), QX[slot], 16,
                           waits=[ln_tp_tok[0], ln_tp_tok[1], t_ktab, t_krope_dve])
                t_nm = P.op("dve", lambda E, sl=slot, c0=c0: E.reduce_sum(
                    small[:, c0:c0 + 1], xch[sl][:, :], axis=AX.X, negate=True), DVE,
                    waits=[t_x, prev_t_xn[slot], prev_t_sq[slot]])
                t_sq = P.op("act", lambda E, sl=slot, c0=c0, sq_b=sq_b: E.activation(
                    sq_b[:, :], xch[sl][:, :], AF.Square,
                    accum_out=small[:, c0 + 2:c0 + 3]),
                    ACTS, waits=[t_x, prev_tp[slot], prev_t_xn[slot], t_krope_dve])
                prev_t_sq[slot] = t_sq
                t_nm = P.op("dve", lambda E, c0=c0: E.tensor_scalar_mul(
                    small[:, c0 + 1:c0 + 2], small[:, c0:c0 + 1], 1.0 / D), DVE, waits=[t_nm])
                t_v = P.op("dve", lambda E, c0=c0: E.tensor_scalar_mul(
                    small[:, c0 + 3:c0 + 4], small[:, c0 + 2:c0 + 3], 1.0 / D),
                    DVE, waits=[t_sq])
                t_m2 = P.op("dve", lambda E, c0=c0: E.tensor_scalar(
                    small[:, c0:c0 + 1], small[:, c0 + 1:c0 + 2],
                    small[:, c0 + 1:c0 + 2], None, ALU.mult), DVE, waits=[t_nm])
                t_v = P.op("dve", lambda E, c0=c0: E.tensor_sub(
                    small[:, c0 + 2:c0 + 3], small[:, c0 + 3:c0 + 4],
                    small[:, c0:c0 + 1]), DVE, waits=[t_v, t_m2])
                t_sd = P.op("act", lambda E, c0=c0: E.activation(
                    small[:, c0 + 4:c0 + 5], small[:, c0 + 2:c0 + 3], AF.Sqrt,
                    bias=eps_sb[:, 0:1]), ACTS, waits=[t_v, t_ones])
                t_rs = P.op("dve", lambda E, c0=c0: E.reciprocal(
                    small[:, c0 + 5:c0 + 6], small[:, c0 + 4:c0 + 5]), DVE, waits=[t_sd])
                t_xnh = [None, None]
                for hfa in range(2):
                    t_xnh[hfa] = P.op("dve", lambda E, sl=slot, c0=c0, sq_b=sq_b, hf=hfa:
                        E.tensor_scalar(
                            sq_b[:, hf * 2048:(hf + 1) * 2048],
                            xch[sl][:, hf * 2048:(hf + 1) * 2048],
                            small[:, c0 + 1:c0 + 2], small[:, c0 + 5:c0 + 6],
                            ALU.add, ALU.mult), DVE, waits=[t_rs, t_sq])
                t_xn = t_xnh[1]
                prev_t_xn[slot] = t_xn
                for g in range(4):
                    tp_tok = None
                    for d8 in range(8):
                        dc = g * 8 + d8
                        tp_tok = P.op("pe", lambda E, dc=dc, d8=d8, sq_b=sq_b: E.transpose(
                            pbT[:, d8 * 128:(d8 + 1) * 128],
                            sq_b[:, dc * 128:(dc + 1) * 128], ident_b[:, :]),
                            PE if d8 == 7 else None,
                            waits=[t_xnh[g // 2], t_idb, ln_tp_tok[0], ln_tp_tok[1]])
                        if d8 == 7:
                            prev_tp[slot] = tp_tok
                    ln_tp_tok[g % 2] = P.op("act", lambda E, g=g, cs=cs: E.copy(
                        xn_span[:, g * 8:(g + 1) * 8, cs * 128:(cs + 1) * 128],
                        pbT[:, 0:1024].rearrange("p (a b) -> p a b", b=128)),
                        ACTS, waits=[tp_tok, xn_spill[sp - 1] if (sp > 0 and g == 0 and cs == 0) else None])
            tok_xn[sp] = (ACTS, ACTS.n)
            xn_spill[sp] = P.op("sp", DMA(xn_scr[sp, :, :],
                                          xn_span[:, :, :].rearrange("p a b -> p (a b)")),
                                QSP, 16, waits=[tok_xn[sp]])

            pa_w = [tok_xn[sp], t_wkr, t_at, t_rt, t_rot1, t_rot2,
                    ln_tp_tok[0], ln_tp_tok[1], t_logit_ev, t_res_ev, t_krope_dve,
                    g_dve_last, tok_cq_done]
            pa_w += [ckv_evac.get((sp - 1, rc)) for rc in range(4)]
            last = None
            grp_tok = {}
            for g in range(16):
                slot = g % 2
                wkv_rd[slot] = P.op("po", DMA(wkv_st[slot][:, :, :],
                                              wkv_bf[:, 2 * g:2 * g + 2, :]), QW[slot], 16,
                                    waits=[grp_tok.get(g - 2, (PE, PE.n) if g < 2 else None),
                                           t_wkvc if g < 2 else None])
                for d2 in range(2):
                    dc = 2 * g + d2
                    st = dc == 0
                    sp_ = dc == 31
                    w = pa_w + [wkv_rd[slot]] if d2 == 0 else [wkv_rd[slot]]
                    for rc in range(4):
                        last = P.op("pe", MM(pb[rc][:, :],
                                             wkv_st[slot][:, d2, rc * 128:(rc + 1) * 128],
                                             xn_span[:, dc, :], st, sp_), None, waits=w)
                        w = []
                    last = P.op("pe", MM(pb[4][0:64, :], wkr_sb[:, dc, :],
                                         xn_span[:, dc, :], st, sp_), None)
                    last = P.op("pe", MM(pb[5][0:64, :], wkr_rot[:, dc, :],
                                         xn_span[:, dc, :], st, sp_), None)
                    last = P.op("pe", MM(pb[6][0:64, :], at_sb[:, dc, :],
                                         xn_span[:, dc, :], st, sp_),
                                 PE if d2 == 1 else None)
                    if d2 == 1:
                        grp_tok[g] = last
            for dc in range(32):
                last = P.op("pe", MM(pb[6][64:128, :], at_sb[:, dc, :],
                                     xn_span[:, dc, :], dc == 0, dc == 31),
                            PE if dc == 31 else None)
            passA = last
            for rc in range(4):
                ckv_evac[(sp, rc)] = P.op("act", lambda E, rc=rc, sp=sp: E.copy(
                    ckv_fm[:, rc, sp * SPAN:(sp + 1) * SPAN], pb[rc][:, :]),
                    ACTS, waits=[passA])
            t1 = P.op("dve", lambda E, sp=sp: E.tensor_mul(
                rop1, pb[4][0:64, :], csk[:, sp * SPAN:(sp + 1) * SPAN]),
                DVE, waits=[passA, t_ktab])
            t2 = P.op("dve", lambda E, sp=sp: E.tensor_mul(
                rop2, pb[5][0:64, :], snk[:, sp * SPAN:(sp + 1) * SPAN]),
                DVE, waits=[passA])
            t_krope_dve = P.op("dve", lambda E, sp=sp: E.tensor_add(
                krope[:, sp * SPAN:(sp + 1) * SPAN], rop1, rop2),
                DVE, waits=[t1, t2])
            t_res_ev = P.op("dve", lambda E: E.tensor_copy(res2[:, :], pb[6][:, :]),
                            DVE, waits=[passA, (DVE, DVE.n)])
            lg_last = None
            for dc in range(32):
                lg_last = P.op("pe", MM(pb[4][0:16, :], rt_sb[:, dc, :],
                                        xn_span[:, dc, :], dc == 0, dc == 31),
                               PE if dc == 31 else None,
                               waits=[t_krope_dve] if dc == 0 else [])
            t_lgfm = P.op("dve", lambda E: E.tensor_copy(lg_fm[0:16, :], pb[4][0:16, :]),
                          DVE, waits=[lg_last])
            tp_lg = None
            for scq in range(4):
                tp_lg = P.op("pe", lambda E, s_=scq: E.transpose(
                    pb[5][:, s_ * 16:(s_ + 1) * 16], lg_fm[0:16, s_ * 128:(s_ + 1) * 128],
                    ident_f[0:16, 0:16]), PE if scq == 3 else None,
                    waits=[t_lgfm, t_id])
            t_logit_ev = P.op("dve", lambda E: E.tensor_copy(lg_sb[:, :], pb[5][:, 0:64]),
                              DVE, waits=[tp_lg])

            rt_tok = {}
            for scq in range(4):
                w0 = [t_logit_ev, (DVE, DVE.n), tp_comb_prev]
                t = P.op("dve", lambda E, s=scq: E.reduce_max(
                    small[:, 12:13], lg_sb[:, s * 16:(s + 1) * 16], axis=AX.X, negate=True),
                    DVE, waits=w0)
                t = P.op("act", lambda E, s=scq: E.activation(
                    e_sb[:, :], lg_sb[:, s * 16:(s + 1) * 16], AF.Exp, bias=small[:, 12:13]),
                    ACTS, waits=[t])
                t = P.op("dve", lambda E: E.reduce_sum(small[:, 13:14], e_sb[:, :], axis=AX.X),
                         DVE, waits=[t])
                t = P.op("dve", lambda E: E.reciprocal(small[:, 14:15], small[:, 13:14]),
                         DVE, waits=[t])
                t = P.op("dve", lambda E: E.tensor_scalar_mul(
                    q_sb[:, :], e_sb[:, :], small[:, 14:15]), DVE, waits=[t])
                t = P.op("dve", lambda E: E.reduce_max(small[:, 15:16], q_sb[:, :], axis=AX.X),
                         DVE, waits=[t])
                t = P.op("dve", lambda E: E.tensor_scalar(
                    ge1_sb[:, :], q_sb[:, :], small[:, 15:16], None, ALU.is_ge),
                    DVE, waits=[t])
                t = P.op("dve", lambda E: E.scalar_tensor_tensor(
                    qm_sb[:, :], ge1_sb[:, :], -1e30, q_sb[:, :], ALU.mult, ALU.add),
                    DVE, waits=[t])
                t = P.op("dve", lambda E: E.reduce_max(small[:, 16:17], qm_sb[:, :], axis=AX.X),
                         DVE, waits=[t])
                t = P.op("dve", lambda E: E.tensor_scalar(
                    ge2_sb[:, :], q_sb[:, :], small[:, 16:17], None, ALU.is_ge),
                    DVE, waits=[t])
                t = P.op("dve", lambda E: E.tensor_scalar(
                    small[:, 17:18], small[:, 16:17], small[:, 15:16], 1e-6, ALU.add, ALU.add),
                    DVE, waits=[t])
                t = P.op("dve", lambda E: E.reciprocal(small[:, 18:19], small[:, 17:18]),
                         DVE, waits=[t])
                t = P.op("dve", lambda E: E.tensor_scalar(
                    small[:, 19:20], small[:, 15:16], small[:, 18:19], 0.5, ALU.mult, ALU.mult),
                    DVE, waits=[t])
                t = P.op("dve", lambda E: E.tensor_scalar(
                    small[:, 20:21], small[:, 16:17], small[:, 18:19], 0.5, ALU.mult, ALU.mult),
                    DVE, waits=[t])
                t = P.op("dve", lambda E: E.tensor_sub(
                    small[:, 21:22], small[:, 19:20], small[:, 20:21]), DVE, waits=[t])
                t = P.op("dve", lambda E: E.tensor_scalar(
                    cmb_sb[:, :], ge2_sb[:, :], small[:, 20:21], None, ALU.mult),
                    DVE, waits=[t])
                t = P.op("dve", lambda E, s=scq: E.scalar_tensor_tensor(
                    cmb2_sb[:, s * 16:(s + 1) * 16], ge1_sb[:, :], small[:, 21:22],
                    cmb_sb[:, :], ALU.mult, ALU.add), DVE, waits=[t, tp_comb_prev])
                rt_tok[scq] = t
            for half in (0,):
                last = None
                grp_tok = {}
                for g in range(16):
                    slot = g % 2
                    wqd_rd[slot] = P.op("po", DMA(
                        wqd_st[slot][:, :, :], wqd_bf[:, 2 * g:2 * g + 2,
                                                      half * 768:(half + 1) * 768]),
                        QW[slot], 16,
                        waits=[grp_tok.get(g - 2, (PE, PE.n) if g < 2 else None),
                               t_wqdc if g < 2 else None])
                    for d2 in range(2):
                        dc = 2 * g + d2
                        st = dc == 0
                        sp_ = dc == 31
                        w = ([wqd_rd[slot], cq_ev, t_lgfm, t_logit_ev, (ACTS, ACTS.n)]
                             if (g == 0 and d2 == 0)
                             else ([wqd_rd[slot]] if d2 == 0 else []))
                        for q6 in range(6):
                            last = P.op("pe", MM(pb[q6][:, :],
                                                 wqd_st[slot][:, d2, q6 * 128:(q6 + 1) * 128],
                                                 xn_span[:, dc, :], st, sp_),
                                        PE if (sp_ or (d2 == 1 and q6 == 5)) else None,
                                        waits=w)
                            w = []
                        if d2 == 1 and q6 == 5:
                            grp_tok[g] = last
                for q6 in range(6):
                    qlc = half * 6 + q6
                    cq_ev = P.op("act", lambda E, qlc=qlc, q6=q6, sp=sp: E.copy(
                        c_q[:, qlc, sp * SPAN:(sp + 1) * SPAN], pb[q6][:, :]),
                        ACTS, waits=[last])
            tp_comb = None
            for scq in range(4):
                tp_comb = P.op("pe", lambda E, s=scq: E.transpose(
                    pb[6][0:16, s * 128:(s + 1) * 128],
                    cmb2_sb[:, s * 16:(s + 1) * 16], ident_f[:, :]),
                    PE if scq == 3 else None,
                    waits=[rt_tok[scq], t_id, t_res_ev])
            tp_comb_prev = tp_comb
            t_combT = P.op("dve", lambda E: E.tensor_copy(combT[0:16, :], pb[6][0:16, :]),
                           DVE, waits=[tp_comb])
            for c in range(8):
                tm1 = P.op("pe", MM(pb[6][0:64, :], ind_sb[:, (2 * c) * 64:(2 * c + 1) * 64],
                                    combT[0:16, :], True, True), None,
                           waits=[t_combT, t_ind, g_dve_last,
                                  g_spill[sp - 1] if (sp > 0 and c == 0) else None])
                tm2 = P.op("pe", MM(pb[6][64:128, :], ind_sb[:, (2 * c + 1) * 64:(2 * c + 2) * 64],
                                    combT[0:16, :], True, True), PE)
                g_dve_last = P.op("dve", lambda E, c=c: E.tensor_mul(
                    g_span[:, c, :], res2[:, :], pb[6][:, :]), DVE,
                    waits=[tm2, g_spill[sp - 1] if (sp > 0 and c == 0) else None])
            g_spill[sp] = P.op("sp", DMA(g_scr[sp, :, :],
                                         g_span[:, :, :].rearrange("p a b -> p (a b)")),
                               QSP, 16, waits=[g_dve_last])
            for half in (1,):
                last = None
                grp_tok = {}
                for g in range(16):
                    slot = g % 2
                    wqd_rd[slot] = P.op("po", DMA(
                        wqd_st[slot][:, :, :], wqd_bf[:, 2 * g:2 * g + 2,
                                                      half * 768:(half + 1) * 768]),
                        QW[slot], 16,
                        waits=[grp_tok.get(g - 2, (PE, PE.n) if g < 2 else None),
                               t_wqdc if g < 2 else None])
                    for d2 in range(2):
                        dc = 2 * g + d2
                        st = dc == 0
                        sp_ = dc == 31
                        w = ([wqd_rd[slot], cq_ev, t_lgfm, t_logit_ev, (ACTS, ACTS.n)]
                             if (g == 0 and d2 == 0)
                             else ([wqd_rd[slot]] if d2 == 0 else []))
                        for q6 in range(6):
                            last = P.op("pe", MM(pb[q6][:, :],
                                                 wqd_st[slot][:, d2, q6 * 128:(q6 + 1) * 128],
                                                 xn_span[:, dc, :], st, sp_),
                                        PE if (sp_ or (d2 == 1 and q6 == 5)) else None,
                                        waits=w)
                            w = []
                        if d2 == 1 and q6 == 5:
                            grp_tok[g] = last
                for q6 in range(6):
                    qlc = half * 6 + q6
                    cq_ev = P.op("act", lambda E, qlc=qlc, q6=q6, sp=sp: E.copy(
                        c_q[:, qlc, sp * SPAN:(sp + 1) * SPAN], pb[q6][:, :]),
                        ACTS, waits=[last])
            tok_cq_done = cq_ev
            passA_last = cq_ev

        P.barrier(ALLSEMS)
        tb2 = []
        for qi, (hi, lo, dstt) in enumerate(((csqh, csql, csq), (snqh, snql, snq))):
            t1 = P.op("sp", DMA(dstt[:, :], hi[:, :]), QT[qi], 16)
            t2 = P.op("sp", DMA(tlo2[:, :], lo[:, :]), QT[qi], 16, waits=tb2[-1:])
            tb2.append(P.op("dve", lambda E, d=dstt: E.tensor_add(d[:, :], d[:, :], tlo2[:, :]),
                            DVE, waits=[t1, t2] + tb2[-1:]))
        t_qtab = tb2[-1]
        t_mask = P.op("po", DMA(mask_sb[:, :, :],
                                maskd[:, :, :].rearrange("v p n -> p v n")), QCP, 16)
        tok_ckvtok = None
        ct_ev = {}
        for tc in range(16):
            for rc in range(4):
                bk = (tc * 4 + rc) % 2
                tt = P.op("pe", lambda E, tc=tc, rc=rc, bk=bk: E.transpose(
                    pbT[:, bk * 128:(bk + 1) * 128],
                    ckv_fm[:, rc, tc * 128:(tc + 1) * 128], ident_b[:, :]),
                    PE, waits=[ct_ev.get(0), ct_ev.get(1)])
                ct_ev[bk] = P.op("dve", lambda E, tc=tc, rc=rc, bk=bk: E.tensor_copy(
                    ckv_tok[:, tc, rc * 128:(rc + 1) * 128], pbT[:, bk * 128:(bk + 1) * 128]),
                    DVE, waits=[tt])
        tok_ckvtok = (DVE, DVE.n)
        if tap:
            P.op("sp", DMA(tap_cq[:, :, :], c_q[:, :, :]), QSP, 16, waits=[tok_cq_done])
            P.op("sp", DMA(tap_ckv[:, :, :], ckv_fm[:, :, :]), QSP, 16)
            P.op("sp", DMA(tap_ckvt[:, :, :], ckv_tok[:, :, :]), QSP, 16, waits=[tok_ckvtok])
            P.op("sp", DMA(tap_kro[:, :], krope[:, :]), QSP, 16)

        wq_rd = [None, None]
        wq_rot = [None, None]
        pe_slot_free = [None, None]
        am_dma = [None, None]

        def load_head(h):
            slot = h % 2
            w = [pe_slot_free[slot]]
            t1 = P.op("po", DMA(wqun_st[slot][:, :], wqun[h, :, :]), QW[slot], 16, waits=w)
            t2 = P.op("po", DMA(
                wqq_st[slot][:, :].rearrange("p (a b) -> p a b", b=128)[:, :, 0:64],
                wqur[h, :, :].rearrange("p (a b) -> p a b", b=DR)), QW[slot], 16)
            t3 = P.op("po", DMA(wuk_st[slot][:, :], wuk[h * 128:(h + 1) * 128, :]), QW[slot], 16)
            t4 = P.op("po", DMA(wuv_st[slot][:, :], wuvT[h, :, :]), QW[slot], 16)
            wq_rd[slot] = (t1, t2, t3, t4)
            r1 = P.op("dve", lambda E, sl=slot: E.tensor_scalar_mul(
                wqq_st[sl][:, :].rearrange("p (a b) -> p a b", b=128)[:, :, 64:96],
                wqq_st[sl][:, :].rearrange("p (a b) -> p a b", b=128)[:, :, 32:64], -1.0),
                DVE, waits=[t2])
            r2 = P.op("dve", lambda E, sl=slot: E.tensor_copy(
                wqq_st[sl][:, :].rearrange("p (a b) -> p a b", b=128)[:, :, 96:128],
                wqq_st[sl][:, :].rearrange("p (a b) -> p a b", b=128)[:, :, 0:32]),
                DVE, waits=[t2])
            wq_rot[slot] = (r1, r2)

        load_head(0)
        t_p3conv = None
        for h in range(H):
            slot = h % 2
            t1, t2, t3, t4 = wq_rd[slot]
            r1, r2 = wq_rot[slot]
            if h + 1 < H:
                load_head(h + 1)
            if h < 16:
                t_p3conv = P.op("po", DMA(bwT_bf[h, :, :], bwT[h, :, :]), QCP, 16)
                t_p3conv = P.op("po", DMA(wout_bf[h, :, :], wout[h, :, :]), QCP, 16)
                t_p3conv = P.op("po", DMA(bst_bf[h, :, :], bst[h, :, :]), QCP, 16)
            qn_ev = None
            qro_dve = None
            ql_ev = {}
            for sp in range(NSPAN):
                w0 = ([t1, tok_cq_done, qn_ev, (ACTS, ACTS.n), (DVE, DVE.n)]
                      if sp == 0 else [qn_ev])
                last = None
                for qlc in range(12):
                    last = P.op("pe", MM(pb[0][:, :], wqun_st[slot][:, qlc * 128:(qlc + 1) * 128],
                                         c_q[:, qlc, sp * SPAN:(sp + 1) * SPAN],
                                         qlc == 0, qlc == 11),
                                PE if qlc == 11 else None, waits=w0)
                    w0 = []
                qn_ev = P.op("act", lambda E, sp=sp: E.activation(
                    qn_sb[:, sp * SPAN:(sp + 1) * SPAN], pb[0][:, :], AF.Copy, scale=SCALE),
                    ACTS, waits=[last])
                w0 = [t2, r1, r2, qro_dve] + ([(ACTS, ACTS.n), (DVE, DVE.n)] if sp == 0 else [])
                for qlc in range(12):
                    last = P.op("pe", MM(pb[1][:, :], wqq_st[slot][:, qlc * 128:(qlc + 1) * 128],
                                         c_q[:, qlc, sp * SPAN:(sp + 1) * SPAN],
                                         qlc == 0, qlc == 11),
                                PE if qlc == 11 else None, waits=w0)
                    w0 = []
                ta = P.op("dve", lambda E, sp=sp: E.tensor_mul(
                    qt1[:, :], pb[1][0:64, :], csq[:, sp * SPAN:(sp + 1) * SPAN]),
                    DVE, waits=[last, t_qtab])
                tb = P.op("dve", lambda E, sp=sp: E.tensor_mul(
                    qt2[:, :], pb[1][64:128, :], snq[:, sp * SPAN:(sp + 1) * SPAN]),
                    DVE, waits=[last])
                qro_dve = P.op("dve", lambda E, sp=sp: E.tensor_add(
                    qro_sb[:, sp * SPAN:(sp + 1) * SPAN], qt1[:, :], qt2[:, :]),
                    DVE, waits=[ta, tb])
            for rc in range(4):
                for sp in range(NSPAN):
                    bk = 3 + (rc * NSPAN + sp) % 4
                    tm = P.op("pe", MM(pb[bk][:, :], wuk_st[slot][:, rc * 128:(rc + 1) * 128],
                                       qn_sb[:, sp * SPAN:(sp + 1) * SPAN], True, True),
                              PE, waits=[t3, qn_ev, ql_ev.get(bk), (DVE, DVE.n)])
                    ql_ev[bk] = P.op("act", lambda E, rc=rc, sp=sp, bk=bk: E.copy(
                        qlat[:, rc, sp * SPAN:(sp + 1) * SPAN], pb[bk][:, :]),
                        ACTS, waits=[tm])
            tok_qsetup = (ACTS, ACTS.n)

            exp_toks = {}
            s_recip_tok = None
            ol_ev = {3: ql_ev.get(3), 4: ql_ev.get(4)}
            sc_ev = {0: None, 1: None}
            bc_ev = None
            am_mul = None
            for qs in range(NSPAN):
                ntc = 4 * qs + 4
                exp_list = []
                for tc in range(ntc):
                    bk = tc % 2
                    w = [tok_qsetup, qro_dve, tok_ckvtok, exp_toks.get(bk), t_mask,
                         ql_ev.get(5), ql_ev.get(6), sc_ev[bk]]
                    last = None
                    for rc in range(4):
                        last = P.op("pe", MM(pb[bk][:, :],
                                             ckv_fm[:, rc, tc * 128:(tc + 1) * 128],
                                             qlat[:, rc, qs * SPAN:(qs + 1) * SPAN],
                                             rc == 0, False), None, waits=w)
                        w = []
                    last = P.op("pe", MM(pb[bk][:, :], krope[:, tc * 128:(tc + 1) * 128],
                                         qro_sb[:, qs * SPAN:(qs + 1) * SPAN],
                                         False, True), PE)
                    sc_ev[bk] = last
                    te = P.op("act", lambda E, tc=tc, bk=bk: E.activation(
                        expT[:, tc, :], pb[bk][:, :], AF.Exp), ACTS, waits=[last])
                    if tc >= 4 * qs:
                        te = P.op("dve", lambda E, tc=tc, v=tc - 4 * qs: E.tensor_mul(
                            expT[:, tc, :], expT[:, tc, :], mask_sb[:, v, :]),
                            DVE, waits=[te])
                    exp_toks[bk] = te
                    exp_list.append(te)
                s_last = None
                for tc in range(ntc):
                    s_last = P.op("pe", MM(pb[6][0:1, :], ones_col[:, 0:1],
                                           expT[:, tc, :], tc == 0, tc == ntc - 1),
                                  PE if tc == ntc - 1 else None,
                                  waits=[exp_list[tc], s_recip_tok])
                s_recip_tok = P.op("dve", lambda E: E.reciprocal(
                    rs_row[0:1, :], pb[6][0:1, :]), DVE, waits=[s_last])
                tbc = None
                for rc in range(4):
                    bk = 3 + rc % 2
                    w = [ol_ev.get(bk)]
                    last = None
                    for tc in range(ntc):
                        last = P.op("pe", MM(pb[bk][:, :],
                                             ckv_tok[:, tc, rc * 128:(rc + 1) * 128],
                                             expT[:, tc, :], tc == 0, tc == ntc - 1),
                                      PE if tc == ntc - 1 else None, waits=w)
                        w = []
                    ol_ev[bk] = P.op("act", lambda E, rc=rc, bk=bk: E.copy(
                        ol_sb[:, rc, :], pb[bk][:, :]), ACTS, waits=[last])
                    if rc == 0:
                        tbc = P.op("pe", MM(pb[2][:, :], ones_l1f[0:1, :], rs_row[0:1, :],
                                            True, True), PE, waits=[s_recip_tok, am_mul])
                bc_ev = P.op("dve", lambda E: E.tensor_copy(rsbc[:, :], pb[2][:, :]),
                             DVE, waits=[tbc])
                w = [ol_ev.get(3), ol_ev.get(4), t4, am_mul]
                last = None
                for rc in range(4):
                    last = P.op("pe", MM(pb[5][:, :], wuv_st[slot][:, rc * 128:(rc + 1) * 128],
                                         ol_sb[:, rc, :], rc == 0, rc == 3),
                                PE if rc == 3 else None, waits=w)
                    w = []
                am_slot = qs % 2
                am_mul = P.op("dve", lambda E, sl=am_slot: E.tensor_mul(
                    am_tile[sl][:, :], pb[5][:, :], rsbc[:, :]),
                    DVE, waits=[last, bc_ev, am_dma[am_slot]])
                am_dma[am_slot] = P.op("sp", DMA(
                    am_scr[qs, :, h * SPAN:(h + 1) * SPAN], am_tile[am_slot][:, :]),
                    QX[am_slot], 16, waits=[am_mul])
            pe_slot_free[slot] = (PE, PE.n)

        P.barrier(ALLSEMS)
        t_bb = P.op("po", DMA(bb_sb[0:1, :], bb[:, :]), QCP, 16)
        w_rd = [None, None]
        pe_w_free = [None, None]
        ep_dve = [None] * 4
        out_dma = [None] * 4
        xr_dma = [None] * 4

        def load_w(qt, osp):
            slot = osp % 2
            w = [(PE, PE.n)]
            a = P.op("po", DMA(bwT_st[slot][:, :, :],
                               bwT_bf[osp, :, :].rearrange("p (a b) -> p a b", b=OSP)),
                     QW[slot], 16, waits=w + [t_p3conv])
            b = P.op("po", DMA(wout_st[slot][:, :, :],
                               wout_bf[osp, :, :].rearrange("p (a b) -> p a b", b=OSP)), QW[slot], 16)
            c = P.op("po", DMA(bst_st[slot][:, :, :],
                               bst_bf[osp, :, :].rearrange("p (a b) -> p a b", b=OSP)), QW[slot], 16)
            w_rd[slot] = (a, b, c)

        for qt in range(NSPAN):
            ta = P.op("sp", DMA(xnq_sb[:, :, :].rearrange("p a b -> p (a b)"),
                                xn_scr[qt, :, :]), QSP, 16,
                      waits=[(PE, PE.n)])
            tb_ = P.op("po", DMA(amq_sb[:, :, :].rearrange("p a b -> p (a b)"),
                                 am_scr[qt, :, :]), QCP, 16, waits=[(PE, PE.n)])
            tg = P.op("po", DMA(gq_sb[:, :, :].rearrange("p a b -> p (a b)"),
                                g_scr[qt, :, :]), QCP, 16)
            if qt == 0:
                load_w(0, 0)
            acts_tok = [ta, tb_, tg]
            for osp in range(NOSP):
                slot = osp % 2
                a, b, c = w_rd[slot]
                if osp + 1 < NOSP:
                    load_w(qt, osp + 1)
                elif qt + 1 < NSPAN:
                    load_w(qt + 1, 0)
                for scq in range(4):
                    bk = scq
                    r0 = qt * SPAN + scq * 128
                    w = acts_tok + [a, b, c, t_bb, ep_dve[bk]]
                    last = None
                    for ic in range(32):
                        last = P.op("pe", MM(pb[bk][:, 0:OSP],
                                             xnq_sb[:, ic, scq * 128:(scq + 1) * 128],
                                             bwT_st[slot][:, ic, :], ic == 0, False),
                                    None, waits=w)
                        w = []
                    for hh in range(32):
                        last = P.op("pe", MM(pb[bk][:, 0:OSP],
                                             amq_sb[:, hh, scq * 128:(scq + 1) * 128],
                                             wout_st[slot][:, hh, :], False, False), None)
                    for cc in range(8):
                        last = P.op("pe", MM(pb[bk][:, 0:OSP],
                                             gq_sb[:, cc, scq * 128:(scq + 1) * 128],
                                             bst_st[slot][:, cc, :], False, False), None)
                    last = P.op("pe", MM(pb[bk][:, 0:OSP], ones_l1b[0:1, :],
                                         bb_sb[0:1, osp * OSP:(osp + 1) * OSP],
                                         False, True), PE)
                    if osp == NOSP - 1:
                        pe_w_free[slot] = (PE, PE.n)
                    xr_dma[bk] = P.op("sp", DMA(xres[bk][:, :],
                                                x[r0:r0 + 128, osp * OSP:(osp + 1) * OSP]),
                                      QX[bk % 2], 16,
                                      waits=[ep_dve[bk], ep_dve[(bk + 2) % 4]])
                    ep_dve[bk] = P.op("dve", lambda E, bk=bk: E.tensor_add(
                        out_sb[bk][:, :], xres[bk][:, :], pb[bk][:, 0:OSP]),
                        DVE, waits=[last, xr_dma[bk], out_dma[bk]])
                    out_dma[bk] = P.op("sp", DMA(
                        out[r0:r0 + 128, osp * OSP:(osp + 1) * OSP], out_sb[bk][:, :]),
                        QX[bk % 2], 16, waits=[ep_dve[bk]])
                if osp == NOSP - 1:
                    pe_w_free[slot] = (PE, PE.n)

        P.op("sp", lambda E: E.wait_ge(QX[0].h, QX[0].n))
        P.op("sp", lambda E: E.wait_ge(QX[1].h, QX[1].n))
        P.op("sp", lambda E: E.wait_ge(QSP.h, QSP.n))

        with nc.Block() as block:

            @block.tensor
            def _(E):
                for f in P.q["pe"]:
                    f(E)

            @block.scalar
            def _(E):
                for f in P.q["act"]:
                    f(E)

            @block.vector
            def _(E):
                for f in P.q["dve"]:
                    f(E)

            @block.gpsimd
            def _(E):
                for f in P.q["po"]:
                    f(E)

            @block.sync
            def _(E):
                for f in P.q["sp"]:
                    f(E)

    return nc


def _aux_inputs():
    f64 = np.float64
    f32 = np.float32
    freqs = (1.0 / 10000.0 ** (np.arange(0, DR, 2, dtype=f32) / f32(DR))).astype(f32)
    wavelengths = (f32(2.0 * np.pi) / freqs).astype(f32)
    ramp = np.clip((wavelengths / f32(4096.0) - f32(1.0)) / f32(31.0),
                   0.0, 1.0).astype(f32)
    inv_freq = (freqs / (f32(1.0) - ramp + ramp * f32(40.0))).astype(f32)
    pos = np.arange(S, dtype=f32)
    ang = (pos[None, :] * inv_freq[:, None]).astype(f32).astype(f64)
    cos2 = np.concatenate([np.cos(ang)] * 2, 0)
    sin2 = np.concatenate([np.sin(ang)] * 2, 0)
    sc = f64(1.0) / np.sqrt(f64(192.0))

    def hilo(t):
        hi = t.astype(np.float32)
        lo = (t - hi.astype(f64)).astype(np.float32)
        return hi, lo

    csqh, csql = hilo(cos2 * sc)
    snqh, snql = hilo(sin2 * sc)
    cskh, cskl = hilo(cos2)
    snkh, snkl = hilo(sin2)
    mask = np.zeros((4, 128, SPAN), np.float32)
    j = np.arange(128)[:, None]
    i = np.arange(SPAN)[None, :]
    for v in range(4):
        mask[v] = (j <= i - 128 * v).astype(np.float32)
    ind = np.zeros((NE, NE * NR), np.float32)
    for e in range(NE):
        ind[e, e * NR:(e + 1) * NR] = 1.0
    return dict(ident=np.eye(128, dtype=np.float32),
                csqh=csqh, csql=csql, snqh=snqh, snql=snql,
                cskh=cskh, cskl=cskl, snkh=snkh, snkl=snkl,
                maskd=mask, indd=ind)


def prepare_weight_maps(L0_attn_w_q_down, L0_attn_w_q_up_nope, L0_attn_w_q_up_rope,
                        L0_attn_w_kv_down, L0_attn_w_k_rope, L0_attn_w_uk,
                        L0_attn_w_uv, L0_attn_w_out, L0_ffn_base_w, L0_ffn_base_b,
                        L0_ffn_A, L0_ffn_B, L0_ffn_router_w):
    c = np.ascontiguousarray
    m = {}
    m["wqd"] = c(L0_attn_w_q_down)
    m["wqun"] = c(L0_attn_w_q_up_nope.reshape(12, 128, H, DN).transpose(2, 1, 0, 3)
                  .reshape(H, 128, QL))
    m["wqur"] = c(L0_attn_w_q_up_rope.reshape(12, 128, H, DR).transpose(2, 1, 0, 3)
                  .reshape(H, 128, 12 * DR))
    m["wkv"] = c(L0_attn_w_kv_down)
    m["wkr"] = c(L0_attn_w_k_rope)
    m["wuk"] = c(L0_attn_w_uk)
    m["wuvT"] = c(L0_attn_w_uv.reshape(H, DV, 4, 128).transpose(0, 3, 2, 1)
                  .reshape(H, 128, 4 * DV))
    m["wout"] = c(L0_attn_w_out.reshape(32, 128, NOSP, OSP).transpose(2, 1, 0, 3)
                  .reshape(NOSP, 128, 32 * OSP))
    m["bwT"] = c(L0_ffn_base_w.T.reshape(32, 128, NOSP, OSP).transpose(2, 1, 0, 3)
                 .reshape(NOSP, 128, 32 * OSP))
    m["bst"] = c(L0_ffn_B.transpose(0, 2, 1).reshape(8, 128, NOSP, OSP)
                 .transpose(2, 1, 0, 3).reshape(NOSP, 128, 8 * OSP))
    m["bb"] = c(L0_ffn_base_b.reshape(1, D))
    m["at_"] = c(L0_ffn_A.T)
    m["rt"] = c(L0_ffn_router_w.T)
    return m


def kernel(L0_attn_hidden_states, L0_attn_w_q_down, L0_attn_w_q_up_nope,
           L0_attn_w_q_up_rope, L0_attn_w_kv_down, L0_attn_w_k_rope, L0_attn_w_uk,
           L0_attn_w_uv, L0_attn_w_out, L0_ffn_base_w, L0_ffn_base_b, L0_ffn_A,
           L0_ffn_B, L0_ffn_router_w):
    hs = np.ascontiguousarray(L0_attn_hidden_states, dtype=np.float32)
    wm = prepare_weight_maps(L0_attn_w_q_down, L0_attn_w_q_up_nope, L0_attn_w_q_up_rope,
                             L0_attn_w_kv_down, L0_attn_w_k_rope, L0_attn_w_uk,
                             L0_attn_w_uv, L0_attn_w_out, L0_ffn_base_w, L0_ffn_base_b,
                             L0_ffn_A, L0_ffn_B, L0_ffn_router_w)
    wm.update(_aux_inputs())
    nc = build_nc()
    in_maps = []
    for cix in range(NCORES):
        im = dict(wm)
        im["x"] = np.ascontiguousarray(hs[cix])
        in_maps.append(im)
    res = run_bass_kernel_spmd(nc, in_maps, core_ids=list(range(NCORES)))
    return np.stack([r["out"] for r in res.results], axis=0)
